# version 19
# baseline (speedup 1.0000x reference)
import numpy as np
import ml_dtypes
from contextlib import ExitStack

import concourse.bass as bass
import concourse.mybir as mybir
from concourse.bass_utils import run_bass_kernel_spmd

F32 = mybir.dt.float32
BF16 = mybir.dt.bfloat16
I32 = mybir.dt.int32
OP = mybir.AluOpType
AF = mybir.ActivationFunctionType
AX = mybir.AxisListType.X

S = 2048
D = 1024
H = 16
DH = 64
RD = 32
NSEQ = 2
TOK = NSEQ * S
INNER = 4096
NCORES = 8
EPS = 1e-5
PAIRSWAP = [i ^ 1 for i in range(32)]

BF = ml_dtypes.bfloat16


def _host_consts():
    inv_freq = (1.0 / (10000.0 ** (np.arange(0, RD, 2, dtype=np.float32)
                                   / np.float32(RD)))).astype(np.float32)
    t = np.arange(S, dtype=np.float32)
    freqs = t[:, None] * inv_freq[None, :]
    rep = np.repeat(freqs, 2, axis=1)
    cosf = np.cos(rep).astype(np.float32)
    sinf = np.sin(rep).astype(np.float32)
    cosx = np.ones((DH, S), np.float32)
    cosx[:RD, :] = cosf.T
    ssin = np.zeros((DH, S), np.float32)
    sgn = np.where(np.arange(RD) % 2 == 0, -1.0, 1.0).astype(np.float32)
    ssin[:RD, :] = (sinf * sgn[None, :]).T
    cospack = np.concatenate([cosx, ssin], axis=0)
    return {
        "eye32": np.eye(128, dtype=np.float32),
        "eyeb": np.eye(128).astype(BF),
        "ones1": np.ones((1, 128), np.float32),
        "onescol": np.ones((128, 1), np.float32),
        "onesrow": np.ones((1, 512), np.float32),
        "cospack": np.ascontiguousarray(cospack),
    }


class Ctr:

    def __init__(self, es, nc, name):
        self.h = es.enter_context(nc.semaphore(name))
        self.n = 0

    def inc(self, instr, k=1):
        instr.then_inc(self.h, k)
        self.n += k


class DmaQ:

    NL = 8

    def __init__(self, es, nc, name):
        self.sems = [es.enter_context(nc.semaphore(name + str(i)))
                     for i in range(self.NL)]
        self.cnt = [0] * self.NL
        self.lane = 0
        self._g = 0

    def group(self, eng):
        self._g += 1
        self.lane = self._g % self.NL
        if self.cnt[self.lane]:
            eng.wait_ge(self.sems[self.lane], self.cnt[self.lane])

    def inc(self, instr, k=16):
        instr.then_inc(self.sems[self.lane], k)
        self.cnt[self.lane] += k

    def tok(self):
        return (self.lane, self.cnt[self.lane])

    def snap(self):
        return tuple(self.cnt)

    def wait(self, eng, tok):
        if tok[1]:
            eng.wait_ge(self.sems[tok[0]], tok[1])

    def wait_snap(self, eng, snap):
        for i, v in enumerate(snap):
            if v:
                eng.wait_ge(self.sems[i], v)


def build_program(tap=None, skip_prep=False):
    tapname = tap[0] if tap else None
    nc = bass.Bass("TRN2", target_bir_lowering=False)

    x = nc.dram_tensor("x", [TOK, D], F32, kind="ExternalInput").ap()
    w_in = nc.dram_tensor("w_in", [D, 3 * D], F32, kind="ExternalInput").ap()
    b_in = nc.dram_tensor("b_in", [3 * D], F32, kind="ExternalInput").ap()
    w_out = nc.dram_tensor("w_out", [D, D], F32, kind="ExternalInput").ap()
    b_out = nc.dram_tensor("b_out", [D], F32, kind="ExternalInput").ap()
    gamma = nc.dram_tensor("gamma", [D], F32, kind="ExternalInput").ap()
    beta = nc.dram_tensor("beta", [D], F32, kind="ExternalInput").ap()
    wg = nc.dram_tensor("wg", [INNER, D], F32, kind="ExternalInput").ap()
    wu = nc.dram_tensor("wu", [INNER, D], F32, kind="ExternalInput").ap()
    wd = nc.dram_tensor("wd", [D, INNER], F32, kind="ExternalInput").ap()
    eye32 = nc.dram_tensor("eye32", [128, 128], F32, kind="ExternalInput").ap()
    eyeb = nc.dram_tensor("eyeb", [128, 128], BF16, kind="ExternalInput").ap()
    ones1 = nc.dram_tensor("ones1", [1, 128], F32, kind="ExternalInput").ap()
    onescol = nc.dram_tensor("onescol", [128, 1], F32, kind="ExternalInput").ap()
    onesrow = nc.dram_tensor("onesrow", [1, 512], F32, kind="ExternalInput").ap()
    cospack = nc.dram_tensor("cospack", [128, S], F32, kind="ExternalInput").ap()
    out = nc.dram_tensor("out", [TOK, D], F32, kind="ExternalOutput").ap()

    xn_s = nc.dram_tensor("xn_s", [TOK, D], F32, kind="Internal").ap()
    s_s = nc.dram_tensor("s_s", [TOK, D], F32, kind="Internal").ap()
    winh_s = nc.dram_tensor("winh_s", [H, 128, 1536], BF16, kind="Internal").ap()
    winl_s = nc.dram_tensor("winl_s", [H, 128, 1536], BF16, kind="Internal").ap()
    wouth_s = nc.dram_tensor("wouth_s", [D, D], BF16, kind="Internal").ap()
    woutl_s = nc.dram_tensor("woutl_s", [D, D], BF16, kind="Internal").ap()
    wgT_s = nc.dram_tensor("wgT_s", [8, 128, 4096], BF16, kind="Internal").ap()
    wuT_s = nc.dram_tensor("wuT_s", [8, 128, 4096], BF16, kind="Internal").ap()
    wdT_s = nc.dram_tensor("wdT_s", [32, 2, 128, 512], BF16, kind="Internal").ap()
    ctxh_s = nc.dram_tensor("ctxh_s", [H, DH, S], BF16, kind="Internal").ap()
    ctxl_s = nc.dram_tensor("ctxl_s", [H, DH, S], BF16, kind="Internal").ap()

    dbg = None
    if tap is not None:
        dbg = nc.dram_tensor("dbg", list(tap[1]),
                             tap[2] if len(tap) > 2 else F32,
                             kind="ExternalOutput").ap()

    es = ExitStack()
    with es:
        LD = DmaQ(es, nc, "LD")
        ST = DmaQ(es, nc, "ST")
        MM = Ctr(es, nc, "MM")
        VV = Ctr(es, nc, "VV")
        AA = Ctr(es, nc, "AA")
        GG = Ctr(es, nc, "GG")
        BAR = Ctr(es, nc, "BAR")

        block = es.enter_context(nc.Block())
        ENGS = ("sync", "gpsimd", "vector", "scalar", "tensor")

        _SELF = {}
        _DRAIN_OPS = {
            "tensor_copy", "tensor_tensor", "tensor_add", "tensor_sub",
            "tensor_mul", "tensor_max", "tensor_scalar", "tensor_single_scalar",
            "tensor_scalar_add", "tensor_scalar_mul", "tensor_scalar_max",
            "tensor_scalar_min", "scalar_tensor_tensor", "tensor_reduce",
            "reduce_max", "reduce_sum", "reciprocal", "stream_shuffle",
            "memset", "memzero", "activation", "select", "copy", "add", "mul",
        }

        class _EngP:
            def __init__(self, eng):
                self._e = eng

            def __getattr__(self, name):
                a = getattr(self._e, name)
                if name in _DRAIN_OPS:
                    def call(*args, _a=a, **kw):
                        _a(*args, **kw)
                        return self._e.drain()
                    return call
                return a

        def sec(name, fn):
            ctr = _SELF.get(name)
            proxy = name in ("vector", "scalar")
            def wrapped(eng, fn=fn, c=ctr, v=(ctr.n if ctr else 0), proxy=proxy):
                if c is not None and v > 0:
                    eng.wait_ge(c.h, v)
                fn(_EngP(eng) if proxy else eng)
            getattr(block, name)(wrapped)

        _SELF.update({"vector": VV, "scalar": AA, "tensor": MM})

        barst = {"r": 0}

        def barrier():
            barst["r"] += 1
            tgt = 5 * barst["r"]
            lds, sts = LD.snap(), ST.snap()
            for e in ENGS:
                def f(eng, tgt=tgt, lds=lds, sts=sts, e=e):
                    if e == "sync":
                        LD.wait_snap(eng, lds)
                    if e == "gpsimd":
                        ST.wait_snap(eng, sts)
                    eng.sem_inc(BAR.h, 1)
                    eng.wait_ge(BAR.h, tgt)
                sec(e, f)
            BAR.n = tgt

        def tap_dump(name, src_ap, wait=()):
            if tapname != name:
                return
            def f(g, src_ap=src_ap, wait=tuple(wait)):
                ST.group(g)
                for ctr, val in wait:
                    g.wait_ge(ctr.h, val)
                ST.inc(g.dma_start(out=dbg[:], in_=src_ap), 16)
            sec("gpsimd", f)

        P0 = es.enter_context(nc.psum_tensor("P0", [128, 1024], F32))
        P1 = es.enter_context(nc.psum_tensor("P1", [128, 1024], F32))
        P1b = es.enter_context(nc.psum_tensor("P1b", [128, 1024], F32))
        P3 = es.enter_context(nc.psum_tensor("P3", [128, 512], F32))
        P2b = es.enter_context(nc.psum_tensor("P2b", [128, 1024], BF16))

        def sb(name, shape, dt=F32, stack=None):
            return (stack or es).enter_context(nc.sbuf_tensor(name, shape, dt))

        t_eye = sb("t_eye", [128, 128])
        t_eyeb = sb("t_eyeb", [128, 128], BF16)
        t_ones1 = sb("t_ones1", [1, 128])
        t_onescol = sb("t_onescol", [128, 1])
        t_onesrow = sb("t_onesrow", [1, 512])
        t_binT = sb("t_binT", [64, 48])
        t_binTsh = sb("t_binTsh", [64, 48])
        t_gammaT = sb("t_gammaT", [128, 8])
        t_betaT = sb("t_betaT", [128, 8])
        t_boutr = sb("t_boutr", [1, 1024])
        t_scal = sb("t_scal", [128, 8])
        t_s11 = sb("t_s11", [1, 8])

        def p0_sync(sync):
            LD.group(sync)
            for dst, src in [(t_eye[:], eye32[:]), (t_eyeb[:], eyeb[:]),
                             (t_ones1[:], ones1[:]), (t_onescol[:], onescol[:]),
                             (t_onesrow[:], onesrow[:]),
                             (t_binT[:], b_in.rearrange("(c p) -> p c", p=64)),
                             (t_gammaT[:], gamma.rearrange("(c p) -> p c", p=128)),
                             (t_betaT[:], beta.rearrange("(c p) -> p c", p=128)),
                             (t_boutr[:], b_out.rearrange("(o d) -> o d", o=1))]:
                LD.inc(sync.dma_start(out=dst, in_=src,
                                      allow_slow_non_contiguous=True), 16)
        sec("sync", p0_sync)

        def p0_vec(vector, ld=LD.tok()):
            LD.wait(vector, ld)
            VV.inc(vector.stream_shuffle(t_binTsh[:], t_binT[:], PAIRSWAP))
        sec("vector", p0_vec)

        PREP_WEIGHTS = [] if skip_prep else [0, 1, 2]
        st1 = ExitStack()
        t_wt = [sb("t_wt%d" % i, [128, 1024], F32, st1) for i in range(3)]
        t_mabs = sb("t_mabs", [128, 1], F32, st1)
        t_mred = sb("t_mred", [128, 1], F32, st1)
        t_wq = [sb("t_wq%d" % i, [128, 1024], BF16, st1) for i in range(2)]
        t_wqa = sb("t_wqa", [128, 1024], F32, st1)
        t_wqb = sb("t_wqb", [128, 1024], F32, st1)
        t_wqT = [sb("t_wqT%d" % i, [128, 1024], BF16, st1) for i in range(2)]

        for wi, (wsrc, R, C, wTdst) in enumerate(
                [(wg, INNER, D, wgT_s), (wu, INNER, D, wuT_s),
                 (wd, D, INNER, wdT_s)]):
            if wi not in PREP_WEIGHTS:
                continue
            colt = C // 1024
            ntile = (R // 128) * colt
            red_t = {}
            for it in range(ntile):
                rt, ct = it // colt, it % colt
                buf = t_wt[it % 3]
                def f_ld(sync, buf=buf, rt=rt, ct=ct, it=it, wsrc=wsrc,
                         vvp=red_t.get(it - 3, 0), vv0=VV.n if it == 0 else 0):
                    LD.group(sync)
                    prev = vvp or vv0
                    if prev:
                        sync.wait_ge(VV.h, prev)
                    LD.inc(sync.dma_start(
                        out=buf[:],
                        in_=wsrc[128 * rt:128 * rt + 128,
                                 1024 * ct:1024 * ct + 1024]), 16)
                sec("sync", f_ld)

                def f_red(vector, buf=buf, it=it, ldt=LD.tok()):
                    LD.wait(vector, ldt)
                    if it == 0:
                        VV.inc(vector.reduce_sum(t_mabs[:], buf[:], AX,
                                                 apply_absolute_value=True))
                    else:
                        vector.reduce_sum(t_mred[:], buf[:], AX,
                                          apply_absolute_value=True)
                        VV.inc(vector.tensor_add(t_mabs[:], t_mabs[:], t_mred[:]))
                sec("vector", f_red)
                red_t[it] = VV.n

            def f_psum(tensor, vvn=VV.n):
                tensor.wait_ge(VV.h, vvn)
                MM.inc(tensor.matmul(P3[0:1, 0:1], t_mabs[:], t_onescol[:],
                                     start=True, stop=True))
            sec("tensor", f_psum)

            def f_scal(vector, mmn=MM.n, R=R, C=C):
                vector.wait_ge(MM.h, mmn)
                vector.tensor_single_scalar(t_s11[0:1, 0:1], P3[0:1, 0:1],
                                            1.0 / (R * C), OP.mult)
                vector.tensor_single_scalar(t_s11[0:1, 1:2], t_s11[0:1, 0:1],
                                            1e-5, OP.max)
                VV.inc(vector.reciprocal(t_s11[0:1, 0:1], t_s11[0:1, 1:2]))
            sec("vector", f_scal)

            def f_rep(tensor, vvn=VV.n):
                tensor.wait_ge(VV.h, vvn)
                MM.inc(tensor.matmul(P1[:, 0:2], t_ones1[:], t_s11[0:1, 0:2],
                                     start=True, stop=True))
            sec("tensor", f_rep)

            def f_repev(vector, mmn=MM.n, wi=wi):
                vector.wait_ge(MM.h, mmn)
                VV.inc(vector.tensor_copy(t_scal[:, 2 * wi:2 * wi + 2], P1[:, 0:2]))
            sec("vector", f_repev)

            q_t = {}
            tr_t = {}
            ev_t = {}
            st_t = {}
            for it in range(ntile):
                rt, ct = it // colt, it % colt
                buf = t_wt[it % 3]
                hp = it % 2
                def f_ld2(sync, buf=buf, rt=rt, ct=ct, it=it, wsrc=wsrc,
                          vvp=q_t.get(it - 3, 0), vv0=VV.n if it == 0 else 0):
                    LD.group(sync)
                    prev = vvp or vv0
                    if prev:
                        sync.wait_ge(VV.h, prev)
                    LD.inc(sync.dma_start(
                        out=buf[:],
                        in_=wsrc[128 * rt:128 * rt + 128,
                                 1024 * ct:1024 * ct + 1024]), 16)
                sec("sync", f_ld2)
                ld_tile = LD.tok()

                def f_q(vector, buf=buf, hp=hp, ldt=ld_tile, wi=wi,
                        mmp=tr_t.get(it - 2, 0), mm0=MM.n if it < 2 else 0):
                    LD.wait(vector, ldt)
                    prev = mmp or mm0
                    if prev:
                        vector.wait_ge(MM.h, prev)
                    vector.tensor_scalar(t_wqa[:], buf[:],
                                         t_scal[:, 2 * wi:2 * wi + 1],
                                         0.5, OP.mult, OP.is_gt)
                    vector.tensor_scalar(t_wqb[:], buf[:],
                                         t_scal[:, 2 * wi:2 * wi + 1],
                                         -0.5, OP.mult, OP.is_lt)
                    VV.inc(vector.tensor_sub(t_wq[hp][:], t_wqa[:], t_wqb[:]))
                sec("vector", f_q)
                q_t[it] = VV.n

                def f_tr(tensor, hp=hp, vvn=VV.n, vve=ev_t.get(it - 1, 0)):
                    tensor.wait_ge(VV.h, max(vvn, vve))
                    for sub in range(8):
                        ins = tensor.transpose(
                            P2b[:, 128 * sub:128 * sub + 128],
                            t_wq[hp][:, 128 * sub:128 * sub + 128], t_eyeb[:])
                    MM.inc(ins)
                sec("tensor", f_tr)
                tr_t[it] = MM.n

                def f_ev(vector, hp=hp, mmn=MM.n, stt=st_t.get(it - 2),
                         sts=ST.snap() if it < 2 else None):
                    vector.wait_ge(MM.h, mmn)
                    if stt:
                        ST.wait(vector, stt)
                    elif sts is not None:
                        ST.wait_snap(vector, sts)
                    VV.inc(vector.tensor_copy(t_wqT[hp][:], P2b[:]))
                sec("vector", f_ev)
                ev_t[it] = VV.n

                def f_st(gpsimd, vvn=VV.n, rt=rt, ct=ct, hp=hp,
                         wTdst=wTdst, wi=wi):
                    ST.group(gpsimd)
                    gpsimd.wait_ge(VV.h, vvn)
                    co = 128 * (rt % 4)
                    for sub in range(8):
                        if wi == 2:
                            dst = wTdst[8 * ct + sub, rt // 4, :, co:co + 128]
                        else:
                            m0 = 512 * sub + co
                            dst = wTdst[rt // 4, :, m0:m0 + 128]
                        ST.inc(gpsimd.dma_start(
                            out=dst,
                            in_=t_wqT[hp][:, 128 * sub:128 * sub + 128]), 16)
                sec("gpsimd", f_st)
                st_t[it] = ST.tok()

        t_wsp = [sb("t_wsp%d" % i, [128, 3072], F32, st1) for i in range(2)]
        t_wsph = [sb("t_wsph%d" % i, [128, 3072], BF16, st1) for i in range(2)]
        t_wspl = [sb("t_wspl%d" % i, [128, 3072], BF16, st1) for i in range(2)]
        w2_st = {}
        w2_vv = {}
        for c in range(8):
            cp = c % 2
            def f_ld(sync, c=c, cp=cp, stt=w2_st.get(c - 2), vvp=w2_vv.get(c - 2, 0)):
                LD.group(sync)
                if stt:
                    ST.wait(sync, stt)
                LD.inc(sync.dma_start(out=t_wsp[cp][:],
                                      in_=w_in[128 * c:128 * c + 128, :]), 16)
            sec("sync", f_ld)

            def f_sp(vector, cp=cp, ldt=LD.tok(), stt=w2_st.get(c - 2)):
                LD.wait(vector, ldt)
                if stt:
                    ST.wait(vector, stt)
                vector.tensor_copy(t_wsph[cp][:], t_wsp[cp][:])
                VV.inc(vector.tensor_sub(t_wspl[cp][:], t_wsp[cp][:], t_wsph[cp][:]))
            sec("vector", f_sp)
            w2_vv[c] = VV.n

            def f_st(gpsimd, c=c, cp=cp, vvn=VV.n):
                ST.group(gpsimd)
                gpsimd.wait_ge(VV.h, vvn)
                for dsts, srcs in ((winh_s, t_wsph[cp]), (winl_s, t_wspl[cp])):
                    src = srcs[:].rearrange("p (q h f) -> p q h f", q=3, f=64)
                    dst = dsts.rearrange("h p (q cb f) -> cb p q h f",
                                         q=3, cb=8)[c]
                    ST.inc(gpsimd.dma_start(out=dst, in_=src), 16)
            sec("gpsimd", f_st)
            w2_st[c] = ST.tok()

        for c in range(8):
            cp = c % 2
            def f_ld(sync, c=c, cp=cp, stt=w2_st.get(6 + c), vvp=0):
                LD.group(sync)
                if stt:
                    ST.wait(sync, stt)
                LD.inc(sync.dma_start(out=t_wsp[cp][:, 0:1024],
                                      in_=w_out[128 * c:128 * c + 128, :]), 16)
            sec("sync", f_ld)

            def f_sp(vector, cp=cp, ldt=LD.tok(), stt=w2_st.get(6 + c)):
                LD.wait(vector, ldt)
                if stt:
                    ST.wait(vector, stt)
                vector.tensor_copy(t_wsph[cp][:, 0:1024], t_wsp[cp][:, 0:1024])
                VV.inc(vector.tensor_sub(t_wspl[cp][:, 0:1024], t_wsp[cp][:, 0:1024],
                                         t_wsph[cp][:, 0:1024]))
            sec("vector", f_sp)

            def f_st(gpsimd, c=c, cp=cp, vvn=VV.n):
                ST.group(gpsimd)
                gpsimd.wait_ge(VV.h, vvn)
                ST.inc(gpsimd.dma_start(out=wouth_s[128 * c:128 * c + 128, :],
                                        in_=t_wsph[cp][:, 0:1024]), 16)
                ST.inc(gpsimd.dma_start(out=woutl_s[128 * c:128 * c + 128, :],
                                        in_=t_wspl[cp][:, 0:1024]), 16)
            sec("gpsimd", f_st)
            w2_st[8 + c] = ST.tok()

        st1.close()
        barrier()

        stA = ExitStack()
        t_cosx = sb("t_cosx", [64, S], F32, stA)
        t_ssin = sb("t_ssin", [64, S], F32, stA)
        t_xn2h = sb("t_xn2h", [128, 8 * S], BF16, stA)
        t_xn2l = sb("t_xn2l", [128, 8 * S], BF16, stA)
        t_Xt = sb("t_Xt", [128, D], F32, stA)
        t_xnt = sb("t_xnt", [128, D], F32, stA)
        t_xn2t = sb("t_xn2t", [128, D], F32, stA)
        t_stats = sb("t_stats", [128, 16], F32, stA)
        t_aff = sb("t_aff", [128, 128], F32, stA)
        t_gamB = sb("t_gamB", [128, 1024], F32, stA)
        t_betB = sb("t_betB", [128, 1024], F32, stA)
        t_winh = sb("t_winh", [128, 1536], BF16, stA)
        t_winl = sb("t_winl", [128, 1536], BF16, stA)
        t_qs = sb("t_qs", [64, 1024], F32, stA)
        t_qr = sb("t_qr", [64, 1024], F32, stA)
        t_qb2 = [sb("t_qb2%d" % i, [128, S], BF16, stA) for i in range(2)]
        t_kb2 = sb("t_kb2", [128, S], BF16, stA)
        t_kb2r = sb("t_kb2r", [128, S], BF16, stA)
        t_rhib = sb("t_rhib", [64, 1024], BF16, stA)
        t_rlob = sb("t_rlob", [64, 1024], BF16, stA)
        t_vnh = sb("t_vnh", [64, 1024], BF16, stA)
        t_vnl = sb("t_vnl", [64, 1024], BF16, stA)
        t_vth = sb("t_vth", [128, 16 * 65], BF16, stA)
        t_vtl = sb("t_vtl", [128, 16 * 65], BF16, stA)
        t_Ef = [sb("t_Ef0", [128, 1024], F32, stA), sb("t_Ef1", [128, 1024], F32, stA)]
        t_Eh = [sb("t_Eh0", [128, 1024], BF16, stA), sb("t_Eh1", [128, 1024], BF16, stA)]
        t_El = [sb("t_El0", [128, 1024], BF16, stA), sb("t_El1", [128, 1024], BF16, stA)]
        t_rc = sb("t_rc", [1, 512], F32, stA)
        t_rr = sb("t_rr", [64, 512], F32, stA)
        t_ctn = sb("t_ctn", [64, 512], F32, stA)
        t_cth = sb("t_cth", [64, S], BF16, stA)
        t_ctl = sb("t_ctl", [64, S], BF16, stA)
        t_wo2h = sb("t_wo2h", [128, 4096], BF16, stA)
        t_wo2l = sb("t_wo2l", [128, 4096], BF16, stA)
        t_xnre = [sb("t_xnre%d" % i, [128, 512], F32, stA) for i in range(2)]
        t_sout = [sb("t_sout%d" % i, [128, 512], F32, stA) for i in range(2)]

        def f_cos(sync):
            LD.group(sync)
            LD.inc(sync.dma_start(out=t_cosx[:], in_=cospack[0:64, :]), 16)
            LD.inc(sync.dma_start(out=t_ssin[:], in_=cospack[64:128, :]), 16)
        sec("sync", f_cos)

        def f_vones(vector):
            va = t_vth[:].rearrange("p (t f) -> p t f", f=65)[:, :, 64:65]
            vector.memset(va, 1.0)
            vb = t_vtl[:].rearrange("p (t f) -> p t f", f=65)[:, :, 64:65]
            VV.inc(vector.memset(vb, 0.0))
        sec("vector", f_vones)

        for seq in range(NSEQ):
            row0 = seq * S

            for t in range(16):
                Xt, xnt = t_Xt, t_xnt
                vv0 = VV.n
                aa0 = AA.n
                def f_ldx(sync, t=t, Xt=Xt, vvn=VV.n):
                    LD.group(sync)
                    if t >= 1 or seq > 0:
                        sync.wait_ge(VV.h, vvn)
                    LD.inc(sync.dma_start(
                        out=Xt[:],
                        in_=x[row0 + 128 * t:row0 + 128 * t + 128, :]), 16)
                sec("sync", f_ldx)
                ld_x = LD.tok()

                def f_act(scalar, Xt=Xt, xnt=xnt, ldt=ld_x, mmn=MM.n, vv0=vv0):
                    LD.wait(scalar, ldt)
                    scalar.wait_ge(MM.h, mmn)
                    AA.inc(scalar.activation(t_xn2t[:], Xt[:], AF.Square,
                                             accum_out=t_stats[:, 1:2]))
                    scalar.wait_ge(VV.h, vv0 + 2)
                    scalar.activation(t_stats[:, 6:7], t_stats[:, 5:6], AF.Ln)
                    AA.inc(scalar.activation(t_stats[:, 7:8], t_stats[:, 6:7],
                                             AF.Exp, scale=-0.5))
                    scalar.wait_ge(VV.h, vv0 + 3)
                    AA.inc(scalar.activation(t_xn2t[:], xnt[:], AF.Square,
                                             accum_out=t_stats[:, 9:10]))
                    scalar.wait_ge(VV.h, vv0 + 5)
                    scalar.activation(t_stats[:, 14:15], t_stats[:, 13:14], AF.Ln)
                    AA.inc(scalar.activation(t_stats[:, 15:16], t_stats[:, 14:15],
                                             AF.Exp, scale=-0.5))
                sec("scalar", f_act)

                def f_v1(vector, Xt=Xt, ldt=ld_x):
                    LD.wait(vector, ldt)
                    VV.inc(vector.reduce_sum(t_stats[:, 0:1], Xt[:], AX))
                sec("vector", f_v1)

                def f_v2(vector, aa0=aa0):
                    vector.wait_ge(AA.h, aa0 + 1)
                    vector.tensor_single_scalar(t_stats[:, 2:3], t_stats[:, 0:1],
                                                1.0 / D, OP.mult)
                    vector.tensor_single_scalar(t_stats[:, 3:4], t_stats[:, 1:2],
                                                1.0 / D, OP.mult)
                    vector.tensor_mul(t_stats[:, 4:5], t_stats[:, 2:3],
                                      t_stats[:, 2:3])
                    vector.tensor_sub(t_stats[:, 5:6], t_stats[:, 3:4],
                                      t_stats[:, 4:5])
                    VV.inc(vector.tensor_single_scalar(
                        t_stats[:, 5:6], t_stats[:, 5:6], EPS, OP.add))
                sec("vector", f_v2)

                def f_v3(vector, Xt=Xt, xnt=xnt, aa0=aa0, sts=ST.snap(), t=t):
                    vector.wait_ge(AA.h, aa0 + 2)
                    if t >= 1 or seq > 0:
                        ST.wait_snap(vector, sts)
                    VV.inc(vector.tensor_scalar(xnt[:], Xt[:], t_stats[:, 2:3],
                                                t_stats[:, 7:8], OP.subtract,
                                                OP.mult))
                sec("vector", f_v3)
                vv_xn = VV.n

                def f_stxn(gpsimd, xnt=xnt, t=t, vvn=vv_xn):
                    ST.group(gpsimd)
                    gpsimd.wait_ge(VV.h, vvn)
                    ST.inc(gpsimd.dma_start(
                        out=xn_s[row0 + 128 * t:row0 + 128 * t + 128, :],
                        in_=xnt[:]), 16)
                sec("gpsimd", f_stxn)

                if t == 0 and seq == 0:
                    tap_dump("xn", t_xnt[:], [(VV, vv_xn)])

                def f_v4(vector, xnt=xnt):
                    VV.inc(vector.reduce_sum(t_stats[:, 8:9], xnt[:], AX))
                sec("vector", f_v4)

                def f_v5(vector, aa0=aa0):
                    vector.wait_ge(AA.h, aa0 + 3)
                    vector.tensor_single_scalar(t_stats[:, 10:11], t_stats[:, 8:9],
                                                1.0 / D, OP.mult)
                    vector.tensor_single_scalar(t_stats[:, 11:12], t_stats[:, 9:10],
                                                1.0 / D, OP.mult)
                    vector.tensor_mul(t_stats[:, 12:13], t_stats[:, 10:11],
                                      t_stats[:, 10:11])
                    vector.tensor_sub(t_stats[:, 13:14], t_stats[:, 11:12],
                                      t_stats[:, 12:13])
                    VV.inc(vector.tensor_single_scalar(
                        t_stats[:, 13:14], t_stats[:, 13:14], EPS, OP.add))
                sec("vector", f_v5)

                def f_v6(vector, xnt=xnt, aa0=aa0, mmn=MM.n):
                    vector.wait_ge(AA.h, aa0 + 4)
                    if mmn:
                        vector.wait_ge(MM.h, mmn)
                    VV.inc(vector.tensor_scalar(t_xn2t[:], xnt[:],
                                                t_stats[:, 10:11],
                                                t_stats[:, 15:16],
                                                OP.subtract, OP.mult))
                sec("vector", f_v6)
                vv_x2 = VV.n
                assert VV.n == vv0 + 6 and AA.n == aa0 + 4

                if seq == 0 and t == 0:
                    def f_gbb(vector):
                        vector.tensor_scalar(t_aff[:], t_eye[:], 0.0, 1.0,
                                             OP.mult, OP.add)
                        for c in range(8):
                            vector.tensor_scalar(
                                t_gamB[:, 128 * c:128 * c + 128], t_aff[:],
                                t_gammaT[:, c:c + 1], None, OP.mult)
                            ins = vector.tensor_scalar(
                                t_betB[:, 128 * c:128 * c + 128], t_aff[:],
                                t_betaT[:, c:c + 1], None, OP.mult)
                        VV.inc(ins)
                    sec("vector", f_gbb)

                def f_tr8(tensor, vvn=vv_x2):
                    tensor.wait_ge(VV.h, vvn)
                    for c in range(8):
                        ins = tensor.transpose(
                            P1[:, 128 * c:128 * c + 128],
                            t_xn2t[:, 128 * c:128 * c + 128], t_eye[:])
                    MM.inc(ins)
                sec("tensor", f_tr8)

                def f_sp(vector, t=t, mmn=MM.n):
                    vector.wait_ge(MM.h, mmn)
                    VV.inc(vector.tensor_mul(t_xn2t[:], P1[:], t_gamB[:]))
                    vector.tensor_add(t_xn2t[:], t_xn2t[:], t_betB[:])
                    srcw = t_xn2t[:].rearrange("p (c j) -> p c j", j=128)
                    dsth = t_xn2h[:].rearrange("p (c s) -> p c s", s=S)[
                        :, :, 128 * t:128 * t + 128]
                    dstl = t_xn2l[:].rearrange("p (c s) -> p c s", s=S)[
                        :, :, 128 * t:128 * t + 128]
                    vector.tensor_copy(dsth, srcw)
                    VV.inc(vector.tensor_sub(dstl, srcw, dsth))
                sec("vector", f_sp)

            tap_dump("xn2h", t_xn2h[:], [(VV, VV.n)])
            tap_dump("xn2l", t_xn2l[:], [(VV, VV.n)])
            barrier()


            def jit_ldw(h, prev_mm):
                def f_ldw(sync, h=h, prev_mm=prev_mm, sts=ST.snap()):
                    LD.group(sync)
                    ST.wait_snap(sync, sts)
                    if prev_mm:
                        sync.wait_ge(MM.h, prev_mm)
                    LD.inc(sync.dma_start(out=t_winh[:], in_=winh_s[h]), 16)
                    LD.inc(sync.dma_start(out=t_winl[:], in_=winl_s[h]), 16)
                sec("sync", f_ldw)
                return LD.tok()

            def jit_half_mm(which, hh, ld_w, extra_vv=0):
                off = 512 * which
                out_mm = []
                def f_mm(tensor, off=off, hh=hh, ldt=ld_w, vvn=VV.n,
                         extra_vv=extra_vv):
                    LD.wait(tensor, ldt)
                    tensor.wait_ge(VV.h, max(vvn, extra_vv))
                    for jj in range(2):
                        base = 1024 * hh + 512 * jj
                        for c in range(8):
                            ps = P0[0:64, 512 * jj:512 * jj + 512]
                            rh = t_xn2h[:, S * c + base:S * c + base + 512]
                            rl = t_xn2l[:, S * c + base:S * c + base + 512]
                            lh = t_winh[:, off + 64 * c:off + 64 * c + 64]
                            ll = t_winl[:, off + 64 * c:off + 64 * c + 64]
                            tensor.matmul(ps, lh, rh, start=(c == 0), stop=False)
                            tensor.matmul(ps, lh, rl, start=False, stop=False)
                            ins = tensor.matmul(ps, ll, rh, start=False,
                                                stop=(c == 7))
                    MM.inc(ins)
                    out_mm.append(MM.n)
                sec("tensor", f_mm)
                return MM.n

            def jit_rope(kind, hh, h, wait_mm):
                bcol = 16 * kind + h
                if kind == 0:
                    dst2, ph, pl = t_qb2[h % 2], 64, 0
                else:
                    dst2, ph, pl = t_kb2, 0, 64
                def f_rope(vector, hh=hh, bcol=bcol, dst2=dst2, ph=ph, pl=pl,
                           wait_mm=wait_mm, kind=kind):
                    c0, c1 = 1024 * hh, 1024 * hh + 1024
                    vector.wait_ge(MM.h, wait_mm)
                    vector.stream_shuffle(t_qs[:], P0[0:64, 0:1024], PAIRSWAP)
                    vector.scalar_tensor_tensor(t_qs[:], t_qs[:],
                                                t_binTsh[:, bcol:bcol + 1],
                                                t_ssin[:, c0:c1], OP.add, OP.mult)
                    vector.scalar_tensor_tensor(t_qr[:], P0[0:64, 0:1024],
                                                t_binT[:, bcol:bcol + 1],
                                                t_cosx[:, c0:c1], OP.add, OP.mult)
                    vector.tensor_add(t_qr[:], t_qr[:], t_qs[:])
                    vector.tensor_copy(t_rhib[:], t_qr[:])
                    vector.tensor_sub(t_rlob[:], t_qr[:], t_rhib[:])
                    vector.tensor_copy(dst2[ph:ph + 64, c0:c1], t_rhib[:])
                    if kind == 1:
                        vector.tensor_copy(dst2[pl:pl + 64, c0:c1], t_rlob[:])
                        vector.tensor_copy(t_kb2r[64:128, c0:c1], t_rhib[:])
                        VV.inc(vector.tensor_copy(t_kb2r[0:64, c0:c1],
                                                  t_rlob[:]))
                    else:
                        VV.inc(vector.tensor_copy(dst2[pl:pl + 64, c0:c1],
                                                  t_rlob[:]))
                sec("vector", f_rope)
                return VV.n

            def jit_v(hh, h, mm_v, ctx_mm):
                def f_vsp(vector, h=h, hh=hh, mmn=mm_v):
                    vector.wait_ge(MM.h, mmn)
                    vector.tensor_scalar(t_qs[:], P0[0:64, 0:1024],
                                         t_binT[:, 32 + h:33 + h], None, OP.add)
                    vector.tensor_copy(t_vnh[:], t_qs[:])
                    VV.inc(vector.tensor_sub(t_vnl[:], t_qs[:], t_vnh[:]))
                sec("vector", f_vsp)
                vv_spl = VV.n
                for vsrc, vdst in ((t_vnh, t_vth), (t_vnl, t_vtl)):
                    def f_vtr(tensor, vsrc=vsrc, vvn=VV.n):
                        tensor.wait_ge(VV.h, vvn)
                        for i8 in range(8):
                            ins = tensor.transpose(
                                P2b[:, 64 * i8:64 * i8 + 64],
                                vsrc[:, 128 * i8:128 * i8 + 128],
                                t_eyeb[0:64, 0:64])
                        MM.inc(ins)
                    sec("tensor", f_vtr)

                    def f_vev(vector, vdst=vdst, hh=hh, mmn=MM.n, ctx_mm=ctx_mm):
                        vector.wait_ge(MM.h, max(mmn, ctx_mm))
                        dst = vdst[:, 65 * 8 * hh:65 * 8 * hh + 520] \
                            .rearrange("p (t f) -> p t f", f=65)[:, :, 0:64]
                        src = P2b[:, 0:512].rearrange("p (t f) -> p t f", f=64)
                        VV.inc(vector.tensor_copy(dst, src))
                    sec("vector", f_vev)
                return VV.n

            def emit_jit_stage(st8, h, state, ctx_mm=0, sc_mm=0):
                if st8 == 0:
                    state["ld_w"] = jit_ldw(h, state.get("last_mm", 0))
                    state["mmqA"] = jit_half_mm(0, 0, state["ld_w"],
                                                state.get("p0_free", 0))
                elif st8 == 1:
                    jit_rope(0, 0, h, state["mmqA"])
                    state["mmqB"] = jit_half_mm(0, 1, state["ld_w"])
                elif st8 == 2:
                    jit_rope(0, 1, h, state["mmqB"])
                    state["mmkA"] = jit_half_mm(1, 0, state["ld_w"])
                elif st8 == 3:
                    def f_kwait(vector, w=sc_mm):
                        pass
                    state["vv_kA"] = jit_rope(1, 0, h, max(state["mmkA"], sc_mm))
                    state["mmkB"] = jit_half_mm(1, 1, state["ld_w"])
                elif st8 == 4:
                    jit_rope(1, 1, h, state["mmkB"])
                    state["mmvA"] = jit_half_mm(2, 0, state["ld_w"])
                elif st8 == 5:
                    jit_v(0, h, max(state["mmvA"], 0), ctx_mm)
                    state["mmvB"] = jit_half_mm(2, 1, state["ld_w"])
                elif st8 == 6:
                    state["p0_free"] = jit_v(1, h, state["mmvB"], ctx_mm)
                    state["last_mm"] = MM.n

            jstate = {}
            for st8 in range(7):
                emit_jit_stage(st8, 0, jstate, ctx_mm=MM.n, sc_mm=MM.n)
            if seq == 0:
                tap_dump("qbh0", t_qb2[0][64:128, :], [(VV, VV.n)])
                tap_dump("kbh0", t_kb2[0:64, :], [(VV, VV.n)])
                tap_dump("vth0", t_vth[:], [(VV, VV.n)])

            for h in range(H):
                nstate = {"last_mm": jstate.get("last_mm", 0),
                          "p0_free": jstate.get("p0_free", 0)}
                aa_h0 = AA.n
                sc_targets = []
                sc_j3g7 = [0]
                ctx_j3g7 = [0]
                for j in range(4):
                    vv0j = VV.n
                    aa0j = aa_h0 + 8 * j
                    mm_sc = {}
                    mm_ctx = {}
                    def f_tens(tensor, j=j, vv0j=vv0j, aa0j=aa0j, h=h,
                               vvh=VV.n, mm_sc=mm_sc, mm_ctx=mm_ctx):
                        for g in range(8):
                            Pg = P1 if g % 2 == 0 else P1b
                            if g < 2:
                                tensor.wait_ge(VV.h, vvh)
                                if aa0j:
                                    tensor.wait_ge(AA.h, aa0j)
                            else:
                                tensor.wait_ge(AA.h, aa0j + g - 1)
                            for k2 in (2 * g, 2 * g + 1):
                                col = 512 * (k2 % 2)
                                ps = Pg[:, col:col + 512]
                                kr = t_kb2r[:, 128 * k2:128 * k2 + 128]
                                k_2 = t_kb2[:, 128 * k2:128 * k2 + 128]
                                q_2 = t_qb2[h % 2][:, 512 * j:512 * j + 512]
                                tensor.matmul(ps, kr, q_2, start=True, stop=False)
                                ins = tensor.matmul(ps, k_2, q_2,
                                                    start=False, stop=True)
                            MM.inc(ins)
                            mm_sc[g] = MM.n
                            if g >= 1:
                                emit_ctx(tensor, j, g - 1, vv0j)
                        emit_ctx(tensor, j, 7, vv0j)
                        tensor.wait_ge(AA.h, aa0j + 8)
                        tensor.wait_ge(VV.h, vv0j + 9)
                        MM.inc(tensor.matmul(P1[0:64, 0:512], t_ones1[0:1, 0:64],
                                             t_rc[:], start=True, stop=True))
                        mm_ctx["rrep"] = MM.n

                    def emit_ctx(tensor, j, g, vv0j):
                        Pg = P1 if g % 2 == 0 else P1b
                        tensor.wait_ge(VV.h, vv0j + g + 1)
                        for k2 in (2 * g, 2 * g + 1):
                            col = 512 * (k2 % 2)
                            vh = t_vth[:, 65 * k2:65 * k2 + 65]
                            vl = t_vtl[:, 65 * k2:65 * k2 + 65]
                            eh = t_Eh[g % 2][:, col:col + 512]
                            el = t_El[g % 2][:, col:col + 512]
                            st = (g == 0 and k2 == 2 * g)
                            sp = (g == 7 and k2 == 2 * g + 1)
                            tensor.matmul(P3[0:65, :], vh, eh, start=st, stop=False)
                            tensor.matmul(P3[0:65, :], vl, eh, start=False, stop=False)
                            ins = tensor.matmul(P3[0:65, :], vh, el,
                                                start=False, stop=sp)
                        MM.inc(ins)
                        mm_ctx[g] = MM.n

                    sec("tensor", f_tens)
                    if j == 3:
                        sc_j3g7[0] = mm_sc[7]
                        ctx_j3g7[0] = mm_ctx[7]

                    for g in range(8):
                        def f_split(vector, g=g, aa0j=aa0j,
                                    mmn=mm_ctx.get(g - 2, 0)):
                            vector.wait_ge(AA.h, aa0j + g + 1)
                            if g >= 2:
                                vector.wait_ge(MM.h, mmn)
                            ep = g % 2
                            vector.tensor_copy(t_Eh[ep][:], t_Ef[ep][:])
                            VV.inc(vector.tensor_sub(t_El[ep][:], t_Ef[ep][:],
                                                     t_Eh[ep][:]))
                        sec("vector", f_split)

                    def f_rc(vector, mmn=mm_ctx[7], sts=ST.snap(), h=h):
                        vector.wait_ge(MM.h, mmn)
                        if h > 0:
                            ST.wait_snap(vector, sts)
                        VV.inc(vector.reciprocal(t_rc[:], P3[64:65, 0:512]))
                    sec("vector", f_rc)

                    def f_norm(vector, j=j, mmn=mm_ctx["rrep"]):
                        vector.wait_ge(MM.h, mmn)
                        vector.tensor_copy(t_rr[:], P1[0:64, 0:512])
                        vector.tensor_mul(t_ctn[:], P3[0:64, 0:512], t_rr[:])
                        vector.tensor_copy(t_cth[:, 512 * j:512 * j + 512],
                                           t_ctn[:])
                        VV.inc(vector.tensor_sub(
                            t_ctl[:, 512 * j:512 * j + 512], t_ctn[:],
                            t_cth[:, 512 * j:512 * j + 512]))
                    sec("vector", f_norm)
                    assert VV.n == vv0j + 10, (VV.n, vv0j)
                    sc_targets.append((j, dict(mm_sc), vv0j))

                    if h + 1 < H:
                        if j == 0:
                            emit_jit_stage(0, h + 1, nstate)
                        elif j == 1:
                            emit_jit_stage(1, h + 1, nstate)
                        elif j == 2:
                            emit_jit_stage(2, h + 1, nstate)

                if h + 1 < H:
                    emit_jit_stage(3, h + 1, nstate, sc_mm=sc_j3g7[0])
                    emit_jit_stage(4, h + 1, nstate)
                    emit_jit_stage(5, h + 1, nstate, ctx_mm=ctx_j3g7[0])
                    emit_jit_stage(6, h + 1, nstate, ctx_mm=ctx_j3g7[0])
                jstate = nstate

                def f_exps(scalar, targets=tuple(sc_targets)):
                    for j, msc, vv0j in targets:
                        for g in range(8):
                            scalar.wait_ge(MM.h, msc[g])
                            if g >= 2:
                                scalar.wait_ge(VV.h, vv0j + g - 1)
                            elif vv0j + g - 3 > 0:
                                scalar.wait_ge(VV.h, vv0j + g - 3)
                            Pg = P1 if g % 2 == 0 else P1b
                            AA.inc(scalar.activation(t_Ef[g % 2][:], Pg[:],
                                                     AF.Exp, scale=0.125))
                sec("scalar", f_exps)
                assert AA.n == aa_h0 + 32, (AA.n, aa_h0)

                def f_stc(gpsimd, h=h, vvn=VV.n):
                    ST.group(gpsimd)
                    gpsimd.wait_ge(VV.h, vvn)
                    ST.inc(gpsimd.dma_start(out=ctxh_s[h], in_=t_cth[:]), 16)
                    ST.inc(gpsimd.dma_start(out=ctxl_s[h], in_=t_ctl[:]), 16)
                sec("gpsimd", f_stc)
                if h == 0 and seq == 0:
                    tap_dump("cth0", t_cth[:], [(VV, VV.n)])

            barrier()

            def f_ldctx(sync, sts=ST.snap()):
                LD.group(sync)
                ST.wait_snap(sync, sts)
                for c in range(8):
                    LD.inc(sync.dma_start(out=t_xn2h[0:64, S * c:S * c + S],
                                          in_=ctxh_s[2 * c]), 16)
                    LD.inc(sync.dma_start(out=t_xn2h[64:128, S * c:S * c + S],
                                          in_=ctxh_s[2 * c + 1]), 16)
                    LD.inc(sync.dma_start(out=t_xn2l[0:64, S * c:S * c + S],
                                          in_=ctxl_s[2 * c]), 16)
                    LD.inc(sync.dma_start(out=t_xn2l[64:128, S * c:S * c + S],
                                          in_=ctxl_s[2 * c + 1]), 16)
            sec("sync", f_ldctx)
            ld_ctx = LD.tok()

            OPS = [P3[0:128, :], P1[:, 0:512]]
            evo_t = {}
            st_o = {}
            for o2 in range(2):
                def f_ldwo(sync, o2=o2, mmn=MM.n):
                    LD.group(sync)
                    if o2 > 0:
                        sync.wait_ge(MM.h, mmn)
                    for dst, srcw in ((t_wo2h, wouth_s), (t_wo2l, woutl_s)):
                        src = srcw[:, 512 * o2:512 * o2 + 512] \
                            .rearrange("(c p) f -> p c f", p=128)
                        LD.inc(sync.dma_start(out=dst[:], in_=src), 16)
                sec("sync", f_ldwo)
                ld_wo = LD.tok()

                for tt in range(16):
                    ot = 16 * o2 + tt
                    par = ot % 2
                    trow = row0 + 128 * tt
                    def f_ldxn(sync, trow=trow, o2=o2, par=par,
                               vvp=evo_t.get(16 * o2 + tt - 2, 0),
                               vv0=VV.n if (16 * o2 + tt) < 2 else 0,
                               sts=ST.snap() if (16 * o2 + tt) < 2 else None):
                        LD.group(sync)
                        prev = vvp or vv0
                        if prev:
                            sync.wait_ge(VV.h, prev)
                        if sts is not None:
                            ST.wait_snap(sync, sts)
                        LD.inc(sync.dma_start(
                            out=t_xnre[par][:],
                            in_=xn_s[trow:trow + 128, 512 * o2:512 * o2 + 512]), 16)
                    sec("sync", f_ldxn)
                    ld_xn = LD.tok()

                    def f_mmo(tensor, tt=tt, o2=o2, par=par, ldn=ld_wo,
                              ldc=ld_ctx, ldx=ld_xn,
                              vvp=evo_t.get(16 * o2 + tt - 2, 0),
                              vv0=VV.n if (16 * o2 + tt) < 2 else 0):
                        LD.wait(tensor, ldn)
                        LD.wait(tensor, ldc)
                        LD.wait(tensor, ldx)
                        prev = vvp or vv0
                        if prev:
                            tensor.wait_ge(VV.h, prev)
                        ps = OPS[par]
                        tensor.matmul(ps, t_ones1[:],
                                      t_boutr[0:1, 512 * o2:512 * o2 + 512],
                                      start=True, stop=False)
                        for c in range(8):
                            lh = t_xn2h[:, S * c + 128 * tt:S * c + 128 * tt + 128]
                            ll = t_xn2l[:, S * c + 128 * tt:S * c + 128 * tt + 128]
                            rh = t_wo2h[:, 512 * c:512 * c + 512]
                            rl = t_wo2l[:, 512 * c:512 * c + 512]
                            tensor.matmul(ps, lh, rh, start=False, stop=False)
                            tensor.matmul(ps, lh, rl, start=False, stop=False)
                            ins = tensor.matmul(ps, ll, rh, start=False,
                                                stop=(c == 7))
                        MM.inc(ins)
                    sec("tensor", f_mmo)

                    def f_evo(vector, par=par, mmn=MM.n, ldx=ld_xn,
                              stp=st_o.get(16 * o2 + tt - 2),
                              sts=ST.snap() if (16 * o2 + tt) < 2 else None):
                        vector.wait_ge(MM.h, mmn)
                        LD.wait(vector, ldx)
                        if stp:
                            ST.wait(vector, stp)
                        elif sts is not None:
                            ST.wait_snap(vector, sts)
                        VV.inc(vector.tensor_add(t_sout[par][:], OPS[par],
                                                 t_xnre[par][:]))
                    sec("vector", f_evo)
                    evo_t[ot] = VV.n

                    def f_sts(gpsimd, trow=trow, o2=o2, par=par, vvn=VV.n):
                        ST.group(gpsimd)
                        gpsimd.wait_ge(VV.h, vvn)
                        ST.inc(gpsimd.dma_start(
                            out=s_s[trow:trow + 128, 512 * o2:512 * o2 + 512],
                            in_=t_sout[par][:]), 16)
                    sec("gpsimd", f_sts)
                    st_o[ot] = ST.tok()

            if seq == 0:
                tap_dump("seq0done", t_sout[1][:], [(VV, VV.n)])
            barrier()

        stA.close()
        barrier()

        stB = ExitStack()
        t_xnc = [sb("t_xnc0", [128, D], F32, stB), sb("t_xnc1", [128, D], F32, stB)]
        t_fq = sb("t_fq", [128, 32], F32, stB)
        t_y = sb("t_y", [128, 2048], F32, stB)
        t_i32 = sb("t_i32", [128, 2048], I32, stB)
        t_c01 = sb("t_c01", [128, 1024], F32, stB)
        t_xq = sb("t_xq", [128, D], BF16, stB)
        t_xqT = sb("t_xqT", [128, 8 * 512], BF16, stB)
        t_h = [sb("t_h%d" % i, [128, INNER], F32, stB) for i in range(4)]
        t_hq = sb("t_hq", [128, INNER], BF16, stB)
        t_hqT = [sb("t_hqT%d" % i, [128, INNER], BF16, stB) for i in range(4)]
        t_wgb = [sb("t_wgb0", [128, 4096], BF16, stB), sb("t_wgb1", [128, 4096], BF16, stB)]
        t_wub = [sb("t_wub0", [128, 4096], BF16, stB), sb("t_wub1", [128, 4096], BF16, stB)]
        t_wdb = [sb("t_wdb%d" % i, [128, 512], BF16, stB) for i in range(4)]
        t_sg = [sb("t_sg0", [128, 512], F32, stB), sb("t_sg1", [128, 512], F32, stB)]
        t_gt = [sb("t_gt0", [128, 512], F32, stB), sb("t_gt1", [128, 512], F32, stB)]
        t_sch = [sb("t_sch0", [128, 512], F32, stB), sb("t_sch1", [128, 512], F32, stB)]
        t_Xch = [sb("t_Xch0", [128, 512], F32, stB), sb("t_Xch1", [128, 512], F32, stB)]
        t_oth = [sb("t_oth0", [128, 512], F32, stB), sb("t_oth1", [128, 512], F32, stB)]


        N_CC = TOK // 512

        def emit_xnquant(cc):
            crow = 512 * cc
            for tt in range(4):
                par = tt % 2
                def f_ldn(sync, tt=tt, par=par, crow=crow, vvn=VV.n, cc=cc):
                    LD.group(sync)
                    if cc > 0 or tt >= 2:
                        sync.wait_ge(VV.h, vvn)
                    LD.inc(sync.dma_start(
                        out=t_xnc[par][:],
                        in_=xn_s[crow + 128 * tt:crow + 128 * tt + 128, :]), 16)
                sec("sync", f_ldn)

                def f_qs(vector, tt=tt, par=par, ldt=LD.tok()):
                    LD.wait(vector, ldt)
                    xc = t_xnc[par][:]
                    cb = 8 * tt
                    vector.reduce_max(t_fq[:, cb:cb + 1], xc, AX,
                                      apply_absolute_value=True)
                    vector.tensor_single_scalar(t_fq[:, cb:cb + 1],
                                                t_fq[:, cb:cb + 1], 1e-5, OP.max)
                    vector.reciprocal(t_fq[:, cb + 1:cb + 2], t_fq[:, cb:cb + 1])
                    vector.tensor_single_scalar(t_fq[:, cb + 1:cb + 2],
                                                t_fq[:, cb + 1:cb + 2], 127.0,
                                                OP.mult)
                    vector.tensor_single_scalar(t_fq[:, cb + 2:cb + 3],
                                                t_fq[:, cb:cb + 1], 1.0 / 127.0,
                                                OP.mult)
                    vector.tensor_mul(t_fq[:, cb + 3:cb + 4], t_fq[:, cb + 2:cb + 3],
                                      t_scal[:, 1:2])
                    VV.inc(vector.tensor_mul(t_fq[:, cb + 4:cb + 5],
                                             t_fq[:, cb + 2:cb + 3],
                                             t_scal[:, 3:4]))
                sec("vector", f_qs)

                qs = 1024 * (tt % 2)
                def f_qa(scalar, tt=tt, par=par, qs=qs, vvn=VV.n):
                    scalar.wait_ge(VV.h, vvn)
                    xc = t_xnc[par][:]
                    cb = 8 * tt
                    scalar.activation(t_y[:, qs:qs + 1024], xc, AF.Copy,
                                      bias=512.0, scale=t_fq[:, cb + 1:cb + 2])
                    AA.inc(scalar.activation(t_i32[:, qs:qs + 1024], xc, AF.Copy,
                                             bias=512.0,
                                             scale=t_fq[:, cb + 1:cb + 2]))
                sec("scalar", f_qa)

                def f_qr(vector, tt=tt, qs=qs, aan=AA.n, mmn=MM.n, cc=cc):
                    vector.wait_ge(AA.h, aan)
                    if cc > 0 or tt > 0:
                        vector.wait_ge(MM.h, mmn)
                    ys = t_y[:, qs:qs + 1024]
                    i32s = t_i32[:, qs:qs + 1024]
                    vector.tensor_sub(ys, ys, i32s)
                    vector.tensor_single_scalar(t_c01[:, 0:1024], ys,
                                                0.5, OP.is_gt)
                    VV.inc(vector.scalar_tensor_tensor(t_xq[:], i32s, -512.0,
                                                       t_c01[:, 0:1024],
                                                       OP.add, OP.add))
                sec("vector", f_qr)
                if cc == 0 and tt == 0:
                    tap_dump("xq0", t_xq[:], [(VV, VV.n)])

                def f_trx(tensor, tt=tt, vvn=VV.n, vve=VV.n):
                    tensor.wait_ge(VV.h, vvn)
                    for c in range(8):
                        ins = tensor.transpose(P2b[:, 128 * c:128 * c + 128],
                                               t_xq[:, 128 * c:128 * c + 128],
                                               t_eyeb[:])
                    MM.inc(ins)
                sec("tensor", f_trx)

                def f_evx(vector, tt=tt, mmn=MM.n):
                    vector.wait_ge(MM.h, mmn)
                    dst = t_xqT[:].rearrange("p (c r) -> p c r", r=512)[
                        :, :, 128 * tt:128 * tt + 128]
                    src = P2b[:].rearrange("p (c r) -> p c r", r=128)
                    VV.inc(vector.tensor_copy(dst, src))
                sec("vector", f_evx)

        emit_xnquant(0)
        prev_gmm = {}
        prev_dmm = {}
        for cc in range(N_CC):
            crow = 512 * cc
            aa_g0 = AA.n
            vv_g0 = VV.n
            g_mm = {}
            for jj in range(8):
                jp = jj % 2
                def f_ldw(sync, jj=jj, jp=jp, cc=cc,
                          mmn=(g_mm[4 * (jj - 2) + 3] if jj >= 2 else
                               prev_gmm.get(jj % 2, 0))):
                    LD.group(sync)
                    if mmn:
                        sync.wait_ge(MM.h, mmn)
                    for wsrc, dstt in ((wgT_s, t_wgb[jp]), (wuT_s, t_wub[jp])):
                        LD.inc(sync.dma_start(out=dstt[:], in_=wsrc[jj]), 16)
                sec("sync", f_ldw)
                ld_wj = LD.tok()

                for tt in range(4):
                    idx = 4 * jj + tt
                    def f_gmm(tensor, jj=jj, jp=jp, tt=tt, idx=idx, ldt=ld_wj,
                              vvn=(VV.n if idx < 2 else 0), vv_g0=vv_g0,
                              aa_g0=aa_g0, aap=AA.n):
                        LD.wait(tensor, ldt)
                        if idx < 2:
                            tensor.wait_ge(VV.h, vvn)
                        else:
                            tensor.wait_ge(VV.h, vv_g0 + idx - 1)
                        if idx >= 2:
                            tensor.wait_ge(AA.h, aa_g0 + idx - 1)
                        elif aap:
                            tensor.wait_ge(AA.h, aap)
                        gps = P3[0:128, :] if tt % 2 == 0 else P0[:, 0:512]
                        for c in range(8):
                            lx = t_xqT[:, 512 * c + 128 * tt:512 * c + 128 * tt + 128]
                            tensor.matmul(gps, lx,
                                          t_wgb[jp][:, 512 * c:512 * c + 512],
                                          start=(c == 0), stop=(c == 7))
                        for c in range(8):
                            lx = t_xqT[:, 512 * c + 128 * tt:512 * c + 128 * tt + 128]
                            ins = tensor.matmul(
                                P1[:, 512 * (tt % 2):512 * (tt % 2) + 512], lx,
                                t_wub[jp][:, 512 * c:512 * c + 512],
                                start=(c == 0), stop=(c == 7))
                        MM.inc(ins)
                    sec("tensor", f_gmm)
                    g_mm[idx] = MM.n

                    def f_h(vector, jj=jj, tt=tt, idx=idx, aa_g0=aa_g0):
                        vector.wait_ge(AA.h, aa_g0 + idx + 1)
                        VV.inc(vector.scalar_tensor_tensor(
                            t_h[tt][:, 512 * jj:512 * jj + 512],
                            P1[:, 512 * (tt % 2):512 * (tt % 2) + 512],
                            t_fq[:, 8 * tt + 4:8 * tt + 5], t_sg[tt % 2][:],
                            OP.mult, OP.mult))
                    sec("vector", f_h)

            prev_gmm = {0: g_mm[4 * 6 + 3], 1: g_mm[4 * 7 + 3]}

            def f_silus(scalar, g_mm=dict(g_mm), vv_g0=vv_g0, aa_g0=aa_g0):
                for idx in range(32):
                    jj, tt = idx // 4, idx % 4
                    scalar.wait_ge(MM.h, g_mm[idx])
                    if idx >= 2:
                        scalar.wait_ge(VV.h, vv_g0 + idx - 1)
                    gps = P3[0:128, :] if tt % 2 == 0 else P0[:, 0:512]
                    AA.inc(scalar.activation(t_sg[tt % 2][:], gps, AF.Silu,
                                             scale=t_fq[:, 8 * tt + 3:8 * tt + 4]))
            sec("scalar", f_silus)
            assert AA.n == aa_g0 + 32
            assert VV.n == vv_g0 + 32

            if cc == 0:
                tap_dump("h00", t_h[0][:], [(VV, VV.n)])

            for tt in range(4):
                def f_hqs(vector, tt=tt, mmn=MM.n, cc=cc, aan=AA.n):
                    cb = 8 * tt
                    if cc > 0:
                        vector.wait_ge(MM.h, mmn)
                    if tt == 0:
                        vector.wait_ge(AA.h, aan)
                    vector.reduce_max(t_fq[:, cb + 5:cb + 6], t_h[tt][:], AX,
                                      apply_absolute_value=True)
                    vector.tensor_single_scalar(t_fq[:, cb + 5:cb + 6],
                                                t_fq[:, cb + 5:cb + 6], 1e-5, OP.max)
                    vector.reciprocal(t_fq[:, cb + 6:cb + 7], t_fq[:, cb + 5:cb + 6])
                    vector.tensor_single_scalar(t_fq[:, cb + 6:cb + 7],
                                                t_fq[:, cb + 6:cb + 7], 127.0,
                                                OP.mult)
                    vector.tensor_single_scalar(t_fq[:, cb + 5:cb + 6],
                                                t_fq[:, cb + 5:cb + 6], 1.0 / 127.0,
                                                OP.mult)
                    VV.inc(vector.tensor_mul(t_fq[:, cb + 7:cb + 8],
                                             t_fq[:, cb + 5:cb + 6],
                                             t_scal[:, 5:6]))
                sec("vector", f_hqs)

                r_t = {}
                for qq in range(4):
                    hs = 1024 * (qq % 2)
                    def f_hqa(scalar, tt=tt, qq=qq, hs=hs, cb=8 * tt,
                              vvn=r_t.get(qq - 2, 0) or VV.n):
                        scalar.wait_ge(VV.h, vvn)
                        hsl = t_h[tt][:, 1024 * qq:1024 * qq + 1024]
                        scalar.activation(t_y[:, hs:hs + 1024], hsl, AF.Copy,
                                          bias=512.0,
                                          scale=t_fq[:, cb + 6:cb + 7])
                        AA.inc(scalar.activation(t_i32[:, hs:hs + 1024], hsl,
                                                 AF.Copy, bias=512.0,
                                                 scale=t_fq[:, cb + 6:cb + 7]))
                    sec("scalar", f_hqa)

                    def f_hqr(vector, tt=tt, qq=qq, hs=hs, aan=AA.n):
                        vector.wait_ge(AA.h, aan)
                        ys = t_y[:, hs:hs + 1024]
                        i32s = t_i32[:, hs:hs + 1024]
                        vector.tensor_sub(ys, ys, i32s)
                        vector.tensor_single_scalar(t_c01[:], ys, 0.5, OP.is_gt)
                        VV.inc(vector.scalar_tensor_tensor(
                            t_hq[:, 1024 * qq:1024 * qq + 1024], i32s,
                            -512.0, t_c01[:], OP.add, OP.add))
                    sec("vector", f_hqr)
                    r_t[qq] = VV.n

                    def f_trh(tensor, tt=tt, qq=qq, vvn=VV.n):
                        tensor.wait_ge(VV.h, vvn)
                        for c in range(8):
                            ci = 8 * qq + c
                            ins = tensor.transpose(P2b[:, 128 * c:128 * c + 128],
                                                   t_hq[:, 128 * ci:128 * ci + 128],
                                                   t_eyeb[:])
                        MM.inc(ins)
                    sec("tensor", f_trh)

                    def f_evh(vector, tt=tt, qq=qq, mmn=MM.n):
                        vector.wait_ge(MM.h, mmn)
                        dst = t_hqT[tt][:, 1024 * qq:1024 * qq + 1024]
                        VV.inc(vector.tensor_copy(dst, P2b[:]))
                    sec("vector", f_evh)
                if cc == 0 and tt == 0:
                    tap_dump("hq0", t_hq[:], [(VV, VV.n)])

            vv_hqT = VV.n
            if cc + 1 < N_CC:
                emit_xnquant(cc + 1)
            DPS = [P0[:, 0:512], P0[:, 512:1024], P1b[:, 0:512], P1b[:, 512:1024]]
            aa_d0 = AA.n
            vv_d0 = VV.n
            d_mm = {}
            dmm_t = {}
            for o in range(2):
                for i in range(32):
                    ip = i % 4
                    pk = (o, i - 4) if i >= 4 else ((o - 1, 28 + i) if o > 0 else None)
                    def f_ldd(sync, o=o, i=i, ip=ip, cc=cc,
                              mmp=(dmm_t.get(pk, 0) if pk else
                                   prev_dmm.get((1, 28 + i), 0))):
                        LD.group(sync)
                        if mmp:
                            sync.wait_ge(MM.h, mmp)
                        LD.inc(sync.dma_start(
                            out=t_wdb[ip][:], in_=wdT_s[i, o]), 16)
                    sec("sync", f_ldd)

                    def f_dmm(tensor, o=o, i=i, ip=ip, ldt=LD.tok(), vvn=vv_hqT,
                              aa_d0=aa_d0, aan=AA.n):
                        LD.wait(tensor, ldt)
                        if i == 0:
                            tensor.wait_ge(VV.h, vvn)
                            if o == 0:
                                tensor.wait_ge(AA.h, aan)
                            else:
                                tensor.wait_ge(AA.h, aa_d0 + 4)
                        for tt in range(4):
                            ins = tensor.matmul(
                                DPS[tt],
                                t_hqT[tt][:, 128 * i:128 * i + 128],
                                t_wdb[ip][:], start=(i == 0), stop=(i == 31))
                        MM.inc(ins)
                    sec("tensor", f_dmm)
                    dmm_t[(o, i)] = MM.n
                d_mm[o] = MM.n

                for tt in range(4):
                    par = tt % 2
                    trow = crow + 128 * tt
                    def f_lds(sync, par=par, trow=trow, o=o, vvn=VV.n, tt=tt,
                              cc=cc, sts=ST.snap()):
                        LD.group(sync)
                        if cc > 0 or o > 0 or tt >= 2:
                            sync.wait_ge(VV.h, vvn)
                        ST.wait_snap(sync, sts)
                        LD.inc(sync.dma_start(
                            out=t_sch[par][:],
                            in_=s_s[trow:trow + 128, 512 * o:512 * o + 512]), 16)
                        LD.inc(sync.dma_start(
                            out=t_Xch[par][:],
                            in_=x[trow:trow + 128, 512 * o:512 * o + 512]), 16)
                    sec("sync", f_lds)
                    ld_sx = LD.tok()

                    def f_comb(vector, par=par, tt=tt, o=o, aa_d0=aa_d0,
                               ldt=ld_sx, sts=ST.snap()):
                        vector.wait_ge(AA.h, aa_d0 + 4 * o + tt + 1)
                        LD.wait(vector, ldt)
                        ST.wait_snap(vector, sts)
                        vector.tensor_mul(t_oth[par][:], t_sch[par][:], t_gt[tt % 2][:])
                        VV.inc(vector.tensor_add(t_oth[par][:], t_oth[par][:],
                                                 t_Xch[par][:]))
                    sec("vector", f_comb)

                    def f_sto(gpsimd, par=par, trow=trow, o=o, vvn=VV.n):
                        ST.group(gpsimd)
                        gpsimd.wait_ge(VV.h, vvn)
                        ST.inc(gpsimd.dma_start(
                            out=out[trow:trow + 128, 512 * o:512 * o + 512],
                            in_=t_oth[par][:]), 16)
                    sec("gpsimd", f_sto)

            prev_dmm = dict(dmm_t)

            def f_dsilu(scalar, d_mm=dict(d_mm), aa_d0=aa_d0, vv_d0=vv_d0, cc=cc):
                for o in range(2):
                    for tt in range(4):
                        scalar.wait_ge(MM.h, d_mm[o])
                        idx = 4 * o + tt
                        if idx >= 2:
                            scalar.wait_ge(VV.h, vv_d0 + 8 * 0 + idx - 1)
                        AA.inc(scalar.activation(
                            t_gt[tt % 2][:], DPS[tt], AF.Silu,
                            scale=t_fq[:, 8 * tt + 7:8 * tt + 8]))
            sec("scalar", f_dsilu)
            assert AA.n == aa_d0 + 8

        barrier()

        stB.close()

        if tapname in ("xns", "sss"):
            src = {"xns": xn_s, "sss": s_s}[tapname]
            def f_tapend(gpsimd, sts=ST.snap(), src=src):
                ST.group(gpsimd)
                ST.wait_snap(gpsimd, sts)
                ST.inc(gpsimd.dma_start(out=dbg[:], in_=src[:]), 16)
            sec("gpsimd", f_tapend)

        def f_final(gpsimd, sts=ST.snap()):
            ST.wait_snap(gpsimd, sts)
        sec("gpsimd", f_final)

    return nc


_CACHE = {}


def kernel(**inputs):
    consts = _host_consts()
    if "nc" not in _CACHE:
        _CACHE["nc"] = build_program()
    nc = _CACHE["nc"]
    X = np.ascontiguousarray(np.asarray(inputs["L0_attn_X"], np.float32))
    base = {
        "w_in": np.ascontiguousarray(np.asarray(inputs["L0_attn_W_in"], np.float32)),
        "b_in": np.ascontiguousarray(np.asarray(inputs["L0_attn_b_in"], np.float32)),
        "w_out": np.ascontiguousarray(np.asarray(inputs["L0_attn_W_out"], np.float32)),
        "b_out": np.ascontiguousarray(np.asarray(inputs["L0_attn_b_out"], np.float32)),
        "gamma": np.ascontiguousarray(np.asarray(inputs["L0_attn_gamma"], np.float32)),
        "beta": np.ascontiguousarray(np.asarray(inputs["L0_attn_beta"], np.float32)),
        "wg": np.ascontiguousarray(np.asarray(inputs["L0_ffn_w_gate"], np.float32)),
        "wu": np.ascontiguousarray(np.asarray(inputs["L0_ffn_w_up"], np.float32)),
        "wd": np.ascontiguousarray(np.asarray(inputs["L0_ffn_w_down"], np.float32)),
    }
    base.update(consts)
    in_maps = []
    for c in range(NCORES):
        m = dict(base)
        m["x"] = np.ascontiguousarray(X[2 * c:2 * c + 2].reshape(TOK, D))
        in_maps.append(m)
    res = run_bass_kernel_spmd(nc, in_maps, core_ids=list(range(NCORES)))
    outs = [np.asarray(r["out"]).reshape(NSEQ, S, D) for r in res.results]
    return np.ascontiguousarray(np.concatenate(outs, axis=0).astype(np.float32))


# version 20 (final)
# speedup vs baseline: 1.0265x; 1.0265x over previous
import numpy as np
import ml_dtypes
from contextlib import ExitStack

import concourse.bass as bass
import concourse.mybir as mybir
from concourse.bass_utils import run_bass_kernel_spmd

F32 = mybir.dt.float32
BF16 = mybir.dt.bfloat16
I32 = mybir.dt.int32
OP = mybir.AluOpType
AF = mybir.ActivationFunctionType
AX = mybir.AxisListType.X

S = 2048
D = 1024
H = 16
DH = 64
RD = 32
NSEQ = 2
TOK = NSEQ * S
INNER = 4096
NCORES = 8
EPS = 1e-5
PAIRSWAP = [i ^ 1 for i in range(32)]

BF = ml_dtypes.bfloat16


def _host_consts():
    inv_freq = (1.0 / (10000.0 ** (np.arange(0, RD, 2, dtype=np.float32)
                                   / np.float32(RD)))).astype(np.float32)
    t = np.arange(S, dtype=np.float32)
    freqs = t[:, None] * inv_freq[None, :]
    rep = np.repeat(freqs, 2, axis=1)
    cosf = np.cos(rep).astype(np.float32)
    sinf = np.sin(rep).astype(np.float32)
    cosx = np.ones((DH, S), np.float32)
    cosx[:RD, :] = cosf.T
    ssin = np.zeros((DH, S), np.float32)
    sgn = np.where(np.arange(RD) % 2 == 0, -1.0, 1.0).astype(np.float32)
    ssin[:RD, :] = (sinf * sgn[None, :]).T
    cospack = np.concatenate([cosx, ssin], axis=0)
    return {
        "eye32": np.eye(128, dtype=np.float32),
        "eyeb": np.eye(128).astype(BF),
        "ones1": np.ones((1, 128), np.float32),
        "onescol": np.ones((128, 1), np.float32),
        "onesrow": np.ones((1, 512), np.float32),
        "cospack": np.ascontiguousarray(cospack),
    }


class Ctr:

    def __init__(self, es, nc, name):
        self.h = es.enter_context(nc.semaphore(name))
        self.n = 0

    def inc(self, instr, k=1):
        instr.then_inc(self.h, k)
        self.n += k


class DmaQ:

    NL = 8

    def __init__(self, es, nc, name):
        self.sems = [es.enter_context(nc.semaphore(name + str(i)))
                     for i in range(self.NL)]
        self.cnt = [0] * self.NL
        self.lane = 0
        self._g = 0

    def group(self, eng):
        self._g += 1
        self.lane = self._g % self.NL
        if self.cnt[self.lane]:
            eng.wait_ge(self.sems[self.lane], self.cnt[self.lane])

    def inc(self, instr, k=16):
        instr.then_inc(self.sems[self.lane], k)
        self.cnt[self.lane] += k

    def tok(self):
        return (self.lane, self.cnt[self.lane])

    def snap(self):
        return tuple(self.cnt)

    def wait(self, eng, tok):
        if tok[1]:
            eng.wait_ge(self.sems[tok[0]], tok[1])

    def wait_snap(self, eng, snap):
        for i, v in enumerate(snap):
            if v:
                eng.wait_ge(self.sems[i], v)


def build_program(tap=None, skip_prep=False):
    tapname = tap[0] if tap else None
    nc = bass.Bass("TRN2", target_bir_lowering=False)

    x = nc.dram_tensor("x", [TOK, D], F32, kind="ExternalInput").ap()
    w_in = nc.dram_tensor("w_in", [D, 3 * D], F32, kind="ExternalInput").ap()
    b_in = nc.dram_tensor("b_in", [3 * D], F32, kind="ExternalInput").ap()
    w_out = nc.dram_tensor("w_out", [D, D], F32, kind="ExternalInput").ap()
    b_out = nc.dram_tensor("b_out", [D], F32, kind="ExternalInput").ap()
    gamma = nc.dram_tensor("gamma", [D], F32, kind="ExternalInput").ap()
    beta = nc.dram_tensor("beta", [D], F32, kind="ExternalInput").ap()
    wg = nc.dram_tensor("wg", [INNER, D], F32, kind="ExternalInput").ap()
    wu = nc.dram_tensor("wu", [INNER, D], F32, kind="ExternalInput").ap()
    wd = nc.dram_tensor("wd", [D, INNER], F32, kind="ExternalInput").ap()
    eye32 = nc.dram_tensor("eye32", [128, 128], F32, kind="ExternalInput").ap()
    eyeb = nc.dram_tensor("eyeb", [128, 128], BF16, kind="ExternalInput").ap()
    ones1 = nc.dram_tensor("ones1", [1, 128], F32, kind="ExternalInput").ap()
    onescol = nc.dram_tensor("onescol", [128, 1], F32, kind="ExternalInput").ap()
    onesrow = nc.dram_tensor("onesrow", [1, 512], F32, kind="ExternalInput").ap()
    cospack = nc.dram_tensor("cospack", [128, S], F32, kind="ExternalInput").ap()
    out = nc.dram_tensor("out", [TOK, D], F32, kind="ExternalOutput").ap()

    xn_s = nc.dram_tensor("xn_s", [TOK, D], F32, kind="Internal").ap()
    s_s = nc.dram_tensor("s_s", [TOK, D], F32, kind="Internal").ap()
    winh_s = nc.dram_tensor("winh_s", [H, 128, 1536], BF16, kind="Internal").ap()
    winl_s = nc.dram_tensor("winl_s", [H, 128, 1536], BF16, kind="Internal").ap()
    wouth_s = nc.dram_tensor("wouth_s", [D, D], BF16, kind="Internal").ap()
    woutl_s = nc.dram_tensor("woutl_s", [D, D], BF16, kind="Internal").ap()
    wgT_s = nc.dram_tensor("wgT_s", [8, 128, 4096], BF16, kind="Internal").ap()
    wuT_s = nc.dram_tensor("wuT_s", [8, 128, 4096], BF16, kind="Internal").ap()
    wdT_s = nc.dram_tensor("wdT_s", [32, 2, 128, 512], BF16, kind="Internal").ap()
    ctxh_s = nc.dram_tensor("ctxh_s", [H, DH, S], BF16, kind="Internal").ap()
    ctxl_s = nc.dram_tensor("ctxl_s", [H, DH, S], BF16, kind="Internal").ap()

    dbg = None
    if tap is not None:
        dbg = nc.dram_tensor("dbg", list(tap[1]),
                             tap[2] if len(tap) > 2 else F32,
                             kind="ExternalOutput").ap()

    es = ExitStack()
    with es:
        LD = DmaQ(es, nc, "LD")
        ST = DmaQ(es, nc, "ST")
        MM = Ctr(es, nc, "MM")
        VV = Ctr(es, nc, "VV")
        AA = Ctr(es, nc, "AA")
        GG = Ctr(es, nc, "GG")
        BAR = Ctr(es, nc, "BAR")

        block = es.enter_context(nc.Block())
        ENGS = ("sync", "gpsimd", "vector", "scalar", "tensor")

        _SELF = {}
        _DRAIN_OPS = {
            "tensor_copy", "tensor_tensor", "tensor_add", "tensor_sub",
            "tensor_mul", "tensor_max", "tensor_scalar", "tensor_single_scalar",
            "tensor_scalar_add", "tensor_scalar_mul", "tensor_scalar_max",
            "tensor_scalar_min", "scalar_tensor_tensor", "tensor_reduce",
            "reduce_max", "reduce_sum", "reciprocal", "stream_shuffle",
            "memset", "memzero", "activation", "select", "copy", "add", "mul",
        }

        class _EngP:
            def __init__(self, eng):
                self._e = eng

            def __getattr__(self, name):
                a = getattr(self._e, name)
                if name in _DRAIN_OPS:
                    def call(*args, _a=a, **kw):
                        _a(*args, **kw)
                        return self._e.drain()
                    return call
                return a

        def sec(name, fn):
            ctr = _SELF.get(name)
            proxy = name in ("vector", "scalar")
            def wrapped(eng, fn=fn, c=ctr, v=(ctr.n if ctr else 0), proxy=proxy):
                if c is not None and v > 0:
                    eng.wait_ge(c.h, v)
                fn(_EngP(eng) if proxy else eng)
            getattr(block, name)(wrapped)

        _SELF.update({"vector": VV, "scalar": AA, "tensor": MM})

        barst = {"r": 0}

        def barrier():
            barst["r"] += 1
            tgt = 5 * barst["r"]
            lds, sts = LD.snap(), ST.snap()
            for e in ENGS:
                def f(eng, tgt=tgt, lds=lds, sts=sts, e=e):
                    if e == "sync":
                        LD.wait_snap(eng, lds)
                    if e == "gpsimd":
                        ST.wait_snap(eng, sts)
                    eng.sem_inc(BAR.h, 1)
                    eng.wait_ge(BAR.h, tgt)
                sec(e, f)
            BAR.n = tgt

        def tap_dump(name, src_ap, wait=()):
            if tapname != name:
                return
            def f(g, src_ap=src_ap, wait=tuple(wait)):
                ST.group(g)
                for ctr, val in wait:
                    g.wait_ge(ctr.h, val)
                ST.inc(g.dma_start(out=dbg[:], in_=src_ap), 16)
            sec("gpsimd", f)

        P0 = es.enter_context(nc.psum_tensor("P0", [128, 1024], F32))
        P1 = es.enter_context(nc.psum_tensor("P1", [128, 1024], F32))
        P1b = es.enter_context(nc.psum_tensor("P1b", [128, 1024], F32))
        P3 = es.enter_context(nc.psum_tensor("P3", [128, 512], F32))
        P2b = es.enter_context(nc.psum_tensor("P2b", [128, 1024], BF16))

        def sb(name, shape, dt=F32, stack=None):
            return (stack or es).enter_context(nc.sbuf_tensor(name, shape, dt))

        t_eye = sb("t_eye", [128, 128])
        t_eyeb = sb("t_eyeb", [128, 128], BF16)
        t_ones1 = sb("t_ones1", [1, 128])
        t_onescol = sb("t_onescol", [128, 1])
        t_onesrow = sb("t_onesrow", [1, 512])
        t_binT = sb("t_binT", [64, 48])
        t_binTsh = sb("t_binTsh", [64, 48])
        t_gammaT = sb("t_gammaT", [128, 8])
        t_betaT = sb("t_betaT", [128, 8])
        t_boutr = sb("t_boutr", [1, 1024])
        t_scal = sb("t_scal", [128, 8])
        t_s11 = sb("t_s11", [1, 8])

        def p0_sync(sync):
            LD.group(sync)
            for dst, src in [(t_eye[:], eye32[:]), (t_eyeb[:], eyeb[:]),
                             (t_ones1[:], ones1[:]), (t_onescol[:], onescol[:]),
                             (t_onesrow[:], onesrow[:]),
                             (t_binT[:], b_in.rearrange("(c p) -> p c", p=64)),
                             (t_gammaT[:], gamma.rearrange("(c p) -> p c", p=128)),
                             (t_betaT[:], beta.rearrange("(c p) -> p c", p=128)),
                             (t_boutr[:], b_out.rearrange("(o d) -> o d", o=1))]:
                LD.inc(sync.dma_start(out=dst, in_=src,
                                      allow_slow_non_contiguous=True), 16)
        sec("sync", p0_sync)

        def p0_vec(vector, ld=LD.tok()):
            LD.wait(vector, ld)
            VV.inc(vector.stream_shuffle(t_binTsh[:], t_binT[:], PAIRSWAP))
        sec("vector", p0_vec)

        PREP_WEIGHTS = [] if skip_prep else [0, 1, 2]
        st1 = ExitStack()
        t_wt = [sb("t_wt%d" % i, [128, 1024], F32, st1) for i in range(4)]
        t_mabs = sb("t_mabs", [128, 1], F32, st1)
        t_mred = sb("t_mred", [128, 1], F32, st1)
        t_wq = [sb("t_wq%d" % i, [128, 1024], BF16, st1) for i in range(2)]
        t_wqa = sb("t_wqa", [128, 1024], F32, st1)
        t_wqb = sb("t_wqb", [128, 1024], F32, st1)
        t_wqT = [sb("t_wqT%d" % i, [128, 1024], BF16, st1) for i in range(2)]

        for wi, (wsrc, R, C, wTdst) in enumerate(
                [(wg, INNER, D, wgT_s), (wu, INNER, D, wuT_s),
                 (wd, D, INNER, wdT_s)]):
            if wi not in PREP_WEIGHTS:
                continue
            colt = C // 1024
            ntile = (R // 128) * colt
            red_t = {}
            for it in range(ntile):
                rt, ct = it // colt, it % colt
                buf = t_wt[it % 4]
                def f_ld(sync, buf=buf, rt=rt, ct=ct, it=it, wsrc=wsrc,
                         vvp=red_t.get(it - 4, 0), vv0=VV.n if it == 0 else 0):
                    LD.group(sync)
                    prev = vvp or vv0
                    if prev:
                        sync.wait_ge(VV.h, prev)
                    LD.inc(sync.dma_start(
                        out=buf[:],
                        in_=wsrc[128 * rt:128 * rt + 128,
                                 1024 * ct:1024 * ct + 1024]), 16)
                sec("sync", f_ld)

                def f_red(vector, buf=buf, it=it, ldt=LD.tok()):
                    LD.wait(vector, ldt)
                    if it == 0:
                        VV.inc(vector.reduce_sum(t_mabs[:], buf[:], AX,
                                                 apply_absolute_value=True))
                    else:
                        vector.reduce_sum(t_mred[:], buf[:], AX,
                                          apply_absolute_value=True)
                        VV.inc(vector.tensor_add(t_mabs[:], t_mabs[:], t_mred[:]))
                sec("vector", f_red)
                red_t[it] = VV.n

            def f_psum(tensor, vvn=VV.n):
                tensor.wait_ge(VV.h, vvn)
                MM.inc(tensor.matmul(P3[0:1, 0:1], t_mabs[:], t_onescol[:],
                                     start=True, stop=True))
            sec("tensor", f_psum)

            def f_scal(vector, mmn=MM.n, R=R, C=C):
                vector.wait_ge(MM.h, mmn)
                vector.tensor_single_scalar(t_s11[0:1, 0:1], P3[0:1, 0:1],
                                            1.0 / (R * C), OP.mult)
                vector.tensor_single_scalar(t_s11[0:1, 1:2], t_s11[0:1, 0:1],
                                            1e-5, OP.max)
                VV.inc(vector.reciprocal(t_s11[0:1, 0:1], t_s11[0:1, 1:2]))
            sec("vector", f_scal)

            def f_rep(tensor, vvn=VV.n):
                tensor.wait_ge(VV.h, vvn)
                MM.inc(tensor.matmul(P1[:, 0:2], t_ones1[:], t_s11[0:1, 0:2],
                                     start=True, stop=True))
            sec("tensor", f_rep)

            def f_repev(vector, mmn=MM.n, wi=wi):
                vector.wait_ge(MM.h, mmn)
                VV.inc(vector.tensor_copy(t_scal[:, 2 * wi:2 * wi + 2], P1[:, 0:2]))
            sec("vector", f_repev)

            q_t = {}
            tr_t = {}
            ev_t = {}
            st_t = {}
            for it in range(ntile):
                rt, ct = it // colt, it % colt
                buf = t_wt[it % 4]
                hp = it % 2
                def f_ld2(sync, buf=buf, rt=rt, ct=ct, it=it, wsrc=wsrc,
                          vvp=q_t.get(it - 4, 0), vv0=VV.n if it == 0 else 0):
                    LD.group(sync)
                    prev = vvp or vv0
                    if prev:
                        sync.wait_ge(VV.h, prev)
                    LD.inc(sync.dma_start(
                        out=buf[:],
                        in_=wsrc[128 * rt:128 * rt + 128,
                                 1024 * ct:1024 * ct + 1024]), 16)
                sec("sync", f_ld2)
                ld_tile = LD.tok()

                def f_q(vector, buf=buf, hp=hp, ldt=ld_tile, wi=wi,
                        mmp=tr_t.get(it - 2, 0), mm0=MM.n if it < 2 else 0):
                    LD.wait(vector, ldt)
                    prev = mmp or mm0
                    if prev:
                        vector.wait_ge(MM.h, prev)
                    vector.tensor_scalar(t_wqa[:], buf[:],
                                         t_scal[:, 2 * wi:2 * wi + 1],
                                         0.5, OP.mult, OP.is_gt)
                    vector.tensor_scalar(t_wqb[:], buf[:],
                                         t_scal[:, 2 * wi:2 * wi + 1],
                                         -0.5, OP.mult, OP.is_lt)
                    VV.inc(vector.tensor_sub(t_wq[hp][:], t_wqa[:], t_wqb[:]))
                sec("vector", f_q)
                q_t[it] = VV.n

                def f_tr(tensor, hp=hp, vvn=VV.n, vve=ev_t.get(it - 1, 0)):
                    tensor.wait_ge(VV.h, max(vvn, vve))
                    for sub in range(8):
                        ins = tensor.transpose(
                            P2b[:, 128 * sub:128 * sub + 128],
                            t_wq[hp][:, 128 * sub:128 * sub + 128], t_eyeb[:])
                    MM.inc(ins)
                sec("tensor", f_tr)
                tr_t[it] = MM.n

                def f_ev(vector, hp=hp, mmn=MM.n, stt=st_t.get(it - 2),
                         sts=ST.snap() if it < 2 else None):
                    vector.wait_ge(MM.h, mmn)
                    if stt:
                        ST.wait(vector, stt)
                    elif sts is not None:
                        ST.wait_snap(vector, sts)
                    VV.inc(vector.tensor_copy(t_wqT[hp][:], P2b[:]))
                sec("vector", f_ev)
                ev_t[it] = VV.n

                def f_st(gpsimd, vvn=VV.n, rt=rt, ct=ct, hp=hp,
                         wTdst=wTdst, wi=wi):
                    ST.group(gpsimd)
                    gpsimd.wait_ge(VV.h, vvn)
                    co = 128 * (rt % 4)
                    for sub in range(8):
                        if wi == 2:
                            dst = wTdst[8 * ct + sub, rt // 4, :, co:co + 128]
                        else:
                            m0 = 512 * sub + co
                            dst = wTdst[rt // 4, :, m0:m0 + 128]
                        ST.inc(gpsimd.dma_start(
                            out=dst,
                            in_=t_wqT[hp][:, 128 * sub:128 * sub + 128]), 16)
                sec("gpsimd", f_st)
                st_t[it] = ST.tok()

        t_wsp = [sb("t_wsp%d" % i, [128, 3072], F32, st1) for i in range(2)]
        t_wsph = [sb("t_wsph%d" % i, [128, 3072], BF16, st1) for i in range(2)]
        t_wspl = [sb("t_wspl%d" % i, [128, 3072], BF16, st1) for i in range(2)]
        w2_st = {}
        w2_vv = {}
        for c in range(8):
            cp = c % 2
            def f_ld(sync, c=c, cp=cp, stt=w2_st.get(c - 2), vvp=w2_vv.get(c - 2, 0)):
                LD.group(sync)
                if stt:
                    ST.wait(sync, stt)
                LD.inc(sync.dma_start(out=t_wsp[cp][:],
                                      in_=w_in[128 * c:128 * c + 128, :]), 16)
            sec("sync", f_ld)

            def f_sp(vector, cp=cp, ldt=LD.tok(), stt=w2_st.get(c - 2)):
                LD.wait(vector, ldt)
                if stt:
                    ST.wait(vector, stt)
                vector.tensor_copy(t_wsph[cp][:], t_wsp[cp][:])
                VV.inc(vector.tensor_sub(t_wspl[cp][:], t_wsp[cp][:], t_wsph[cp][:]))
            sec("vector", f_sp)
            w2_vv[c] = VV.n

            def f_st(gpsimd, c=c, cp=cp, vvn=VV.n):
                ST.group(gpsimd)
                gpsimd.wait_ge(VV.h, vvn)
                for dsts, srcs in ((winh_s, t_wsph[cp]), (winl_s, t_wspl[cp])):
                    src = srcs[:].rearrange("p (q h f) -> p q h f", q=3, f=64)
                    dst = dsts.rearrange("h p (q cb f) -> cb p q h f",
                                         q=3, cb=8)[c]
                    ST.inc(gpsimd.dma_start(out=dst, in_=src), 16)
            sec("gpsimd", f_st)
            w2_st[c] = ST.tok()

        for c in range(8):
            cp = c % 2
            def f_ld(sync, c=c, cp=cp, stt=w2_st.get(6 + c), vvp=0):
                LD.group(sync)
                if stt:
                    ST.wait(sync, stt)
                LD.inc(sync.dma_start(out=t_wsp[cp][:, 0:1024],
                                      in_=w_out[128 * c:128 * c + 128, :]), 16)
            sec("sync", f_ld)

            def f_sp(vector, cp=cp, ldt=LD.tok(), stt=w2_st.get(6 + c)):
                LD.wait(vector, ldt)
                if stt:
                    ST.wait(vector, stt)
                vector.tensor_copy(t_wsph[cp][:, 0:1024], t_wsp[cp][:, 0:1024])
                VV.inc(vector.tensor_sub(t_wspl[cp][:, 0:1024], t_wsp[cp][:, 0:1024],
                                         t_wsph[cp][:, 0:1024]))
            sec("vector", f_sp)

            def f_st(gpsimd, c=c, cp=cp, vvn=VV.n):
                ST.group(gpsimd)
                gpsimd.wait_ge(VV.h, vvn)
                ST.inc(gpsimd.dma_start(out=wouth_s[128 * c:128 * c + 128, :],
                                        in_=t_wsph[cp][:, 0:1024]), 16)
                ST.inc(gpsimd.dma_start(out=woutl_s[128 * c:128 * c + 128, :],
                                        in_=t_wspl[cp][:, 0:1024]), 16)
            sec("gpsimd", f_st)
            w2_st[8 + c] = ST.tok()

        st1.close()
        barrier()

        stA = ExitStack()
        t_cosx = sb("t_cosx", [64, S], F32, stA)
        t_ssin = sb("t_ssin", [64, S], F32, stA)
        t_xn2h = sb("t_xn2h", [128, 8 * S], BF16, stA)
        t_xn2l = sb("t_xn2l", [128, 8 * S], BF16, stA)
        t_Xt = sb("t_Xt", [128, D], F32, stA)
        t_xnt = sb("t_xnt", [128, D], F32, stA)
        t_xn2t = sb("t_xn2t", [128, D], F32, stA)
        t_stats = sb("t_stats", [128, 16], F32, stA)
        t_aff = sb("t_aff", [128, 128], F32, stA)
        t_gamB = sb("t_gamB", [128, 1024], F32, stA)
        t_betB = sb("t_betB", [128, 1024], F32, stA)
        t_winh = sb("t_winh", [128, 1536], BF16, stA)
        t_winl = sb("t_winl", [128, 1536], BF16, stA)
        t_qs = sb("t_qs", [64, 1024], F32, stA)
        t_qr = sb("t_qr", [64, 1024], F32, stA)
        t_qb2 = [sb("t_qb2%d" % i, [128, S], BF16, stA) for i in range(2)]
        t_kb2 = sb("t_kb2", [128, S], BF16, stA)
        t_kb2r = sb("t_kb2r", [128, S], BF16, stA)
        t_rhib = sb("t_rhib", [64, 1024], BF16, stA)
        t_rlob = sb("t_rlob", [64, 1024], BF16, stA)
        t_vnh = sb("t_vnh", [64, 1024], BF16, stA)
        t_vnl = sb("t_vnl", [64, 1024], BF16, stA)
        t_vth = sb("t_vth", [128, 16 * 65], BF16, stA)
        t_vtl = sb("t_vtl", [128, 16 * 65], BF16, stA)
        t_Ef = [sb("t_Ef0", [128, 1024], F32, stA), sb("t_Ef1", [128, 1024], F32, stA)]
        t_Eh = [sb("t_Eh0", [128, 1024], BF16, stA), sb("t_Eh1", [128, 1024], BF16, stA)]
        t_El = [sb("t_El0", [128, 1024], BF16, stA), sb("t_El1", [128, 1024], BF16, stA)]
        t_rc = sb("t_rc", [1, 512], F32, stA)
        t_rr = sb("t_rr", [64, 512], F32, stA)
        t_ctn = sb("t_ctn", [64, 512], F32, stA)
        t_cth = sb("t_cth", [64, S], BF16, stA)
        t_ctl = sb("t_ctl", [64, S], BF16, stA)
        t_wo2h = sb("t_wo2h", [128, 4096], BF16, stA)
        t_wo2l = sb("t_wo2l", [128, 4096], BF16, stA)
        t_xnre = [sb("t_xnre%d" % i, [128, 512], F32, stA) for i in range(2)]
        t_sout = [sb("t_sout%d" % i, [128, 512], F32, stA) for i in range(2)]

        def f_cos(sync):
            LD.group(sync)
            LD.inc(sync.dma_start(out=t_cosx[:], in_=cospack[0:64, :]), 16)
            LD.inc(sync.dma_start(out=t_ssin[:], in_=cospack[64:128, :]), 16)
        sec("sync", f_cos)

        def f_vones(vector):
            va = t_vth[:].rearrange("p (t f) -> p t f", f=65)[:, :, 64:65]
            vector.memset(va, 1.0)
            vb = t_vtl[:].rearrange("p (t f) -> p t f", f=65)[:, :, 64:65]
            VV.inc(vector.memset(vb, 0.0))
        sec("vector", f_vones)

        for seq in range(NSEQ):
            row0 = seq * S

            for t in range(16):
                Xt, xnt = t_Xt, t_xnt
                vv0 = VV.n
                aa0 = AA.n
                def f_ldx(sync, t=t, Xt=Xt, vvn=VV.n):
                    LD.group(sync)
                    if t >= 1 or seq > 0:
                        sync.wait_ge(VV.h, vvn)
                    LD.inc(sync.dma_start(
                        out=Xt[:],
                        in_=x[row0 + 128 * t:row0 + 128 * t + 128, :]), 16)
                sec("sync", f_ldx)
                ld_x = LD.tok()

                def f_act(scalar, Xt=Xt, xnt=xnt, ldt=ld_x, mmn=MM.n, vv0=vv0):
                    LD.wait(scalar, ldt)
                    scalar.wait_ge(MM.h, mmn)
                    AA.inc(scalar.activation(t_xn2t[:], Xt[:], AF.Square,
                                             accum_out=t_stats[:, 1:2]))
                    scalar.wait_ge(VV.h, vv0 + 2)
                    scalar.activation(t_stats[:, 6:7], t_stats[:, 5:6], AF.Ln)
                    AA.inc(scalar.activation(t_stats[:, 7:8], t_stats[:, 6:7],
                                             AF.Exp, scale=-0.5))
                    scalar.wait_ge(VV.h, vv0 + 3)
                    AA.inc(scalar.activation(t_xn2t[:], xnt[:], AF.Square,
                                             accum_out=t_stats[:, 9:10]))
                    scalar.wait_ge(VV.h, vv0 + 5)
                    scalar.activation(t_stats[:, 14:15], t_stats[:, 13:14], AF.Ln)
                    AA.inc(scalar.activation(t_stats[:, 15:16], t_stats[:, 14:15],
                                             AF.Exp, scale=-0.5))
                sec("scalar", f_act)

                def f_v1(vector, Xt=Xt, ldt=ld_x):
                    LD.wait(vector, ldt)
                    VV.inc(vector.reduce_sum(t_stats[:, 0:1], Xt[:], AX))
                sec("vector", f_v1)

                def f_v2(vector, aa0=aa0):
                    vector.wait_ge(AA.h, aa0 + 1)
                    vector.tensor_single_scalar(t_stats[:, 2:3], t_stats[:, 0:1],
                                                1.0 / D, OP.mult)
                    vector.tensor_single_scalar(t_stats[:, 3:4], t_stats[:, 1:2],
                                                1.0 / D, OP.mult)
                    vector.tensor_mul(t_stats[:, 4:5], t_stats[:, 2:3],
                                      t_stats[:, 2:3])
                    vector.tensor_sub(t_stats[:, 5:6], t_stats[:, 3:4],
                                      t_stats[:, 4:5])
                    VV.inc(vector.tensor_single_scalar(
                        t_stats[:, 5:6], t_stats[:, 5:6], EPS, OP.add))
                sec("vector", f_v2)

                def f_v3(vector, Xt=Xt, xnt=xnt, aa0=aa0, sts=ST.snap(), t=t):
                    vector.wait_ge(AA.h, aa0 + 2)
                    if t >= 1 or seq > 0:
                        ST.wait_snap(vector, sts)
                    VV.inc(vector.tensor_scalar(xnt[:], Xt[:], t_stats[:, 2:3],
                                                t_stats[:, 7:8], OP.subtract,
                                                OP.mult))
                sec("vector", f_v3)
                vv_xn = VV.n

                def f_stxn(gpsimd, xnt=xnt, t=t, vvn=vv_xn):
                    ST.group(gpsimd)
                    gpsimd.wait_ge(VV.h, vvn)
                    ST.inc(gpsimd.dma_start(
                        out=xn_s[row0 + 128 * t:row0 + 128 * t + 128, :],
                        in_=xnt[:]), 16)
                sec("gpsimd", f_stxn)

                if t == 0 and seq == 0:
                    tap_dump("xn", t_xnt[:], [(VV, vv_xn)])

                def f_v4(vector, xnt=xnt):
                    VV.inc(vector.reduce_sum(t_stats[:, 8:9], xnt[:], AX))
                sec("vector", f_v4)

                def f_v5(vector, aa0=aa0):
                    vector.wait_ge(AA.h, aa0 + 3)
                    vector.tensor_single_scalar(t_stats[:, 10:11], t_stats[:, 8:9],
                                                1.0 / D, OP.mult)
                    vector.tensor_single_scalar(t_stats[:, 11:12], t_stats[:, 9:10],
                                                1.0 / D, OP.mult)
                    vector.tensor_mul(t_stats[:, 12:13], t_stats[:, 10:11],
                                      t_stats[:, 10:11])
                    vector.tensor_sub(t_stats[:, 13:14], t_stats[:, 11:12],
                                      t_stats[:, 12:13])
                    VV.inc(vector.tensor_single_scalar(
                        t_stats[:, 13:14], t_stats[:, 13:14], EPS, OP.add))
                sec("vector", f_v5)

                def f_v6(vector, xnt=xnt, aa0=aa0, mmn=MM.n):
                    vector.wait_ge(AA.h, aa0 + 4)
                    if mmn:
                        vector.wait_ge(MM.h, mmn)
                    VV.inc(vector.tensor_scalar(t_xn2t[:], xnt[:],
                                                t_stats[:, 10:11],
                                                t_stats[:, 15:16],
                                                OP.subtract, OP.mult))
                sec("vector", f_v6)
                vv_x2 = VV.n
                assert VV.n == vv0 + 6 and AA.n == aa0 + 4

                if seq == 0 and t == 0:
                    def f_gbb(vector):
                        vector.tensor_scalar(t_aff[:], t_eye[:], 0.0, 1.0,
                                             OP.mult, OP.add)
                        for c in range(8):
                            vector.tensor_scalar(
                                t_gamB[:, 128 * c:128 * c + 128], t_aff[:],
                                t_gammaT[:, c:c + 1], None, OP.mult)
                            ins = vector.tensor_scalar(
                                t_betB[:, 128 * c:128 * c + 128], t_aff[:],
                                t_betaT[:, c:c + 1], None, OP.mult)
                        VV.inc(ins)
                    sec("vector", f_gbb)

                def f_tr8(tensor, vvn=vv_x2):
                    tensor.wait_ge(VV.h, vvn)
                    for c in range(8):
                        ins = tensor.transpose(
                            P1[:, 128 * c:128 * c + 128],
                            t_xn2t[:, 128 * c:128 * c + 128], t_eye[:])
                    MM.inc(ins)
                sec("tensor", f_tr8)

                def f_sp(vector, t=t, mmn=MM.n):
                    vector.wait_ge(MM.h, mmn)
                    VV.inc(vector.tensor_mul(t_xn2t[:], P1[:], t_gamB[:]))
                    vector.tensor_add(t_xn2t[:], t_xn2t[:], t_betB[:])
                    srcw = t_xn2t[:].rearrange("p (c j) -> p c j", j=128)
                    dsth = t_xn2h[:].rearrange("p (c s) -> p c s", s=S)[
                        :, :, 128 * t:128 * t + 128]
                    dstl = t_xn2l[:].rearrange("p (c s) -> p c s", s=S)[
                        :, :, 128 * t:128 * t + 128]
                    vector.tensor_copy(dsth, srcw)
                    VV.inc(vector.tensor_sub(dstl, srcw, dsth))
                sec("vector", f_sp)

            tap_dump("xn2h", t_xn2h[:], [(VV, VV.n)])
            tap_dump("xn2l", t_xn2l[:], [(VV, VV.n)])
            barrier()


            def jit_ldw(h, prev_mm):
                def f_ldw(sync, h=h, prev_mm=prev_mm, sts=ST.snap()):
                    LD.group(sync)
                    ST.wait_snap(sync, sts)
                    if prev_mm:
                        sync.wait_ge(MM.h, prev_mm)
                    LD.inc(sync.dma_start(out=t_winh[:], in_=winh_s[h]), 16)
                    LD.inc(sync.dma_start(out=t_winl[:], in_=winl_s[h]), 16)
                sec("sync", f_ldw)
                return LD.tok()

            def jit_half_mm(which, hh, ld_w, extra_vv=0):
                off = 512 * which
                out_mm = []
                def f_mm(tensor, off=off, hh=hh, ldt=ld_w, vvn=VV.n,
                         extra_vv=extra_vv):
                    LD.wait(tensor, ldt)
                    tensor.wait_ge(VV.h, max(vvn, extra_vv))
                    for jj in range(2):
                        base = 1024 * hh + 512 * jj
                        for c in range(8):
                            ps = P0[0:64, 512 * jj:512 * jj + 512]
                            rh = t_xn2h[:, S * c + base:S * c + base + 512]
                            rl = t_xn2l[:, S * c + base:S * c + base + 512]
                            lh = t_winh[:, off + 64 * c:off + 64 * c + 64]
                            ll = t_winl[:, off + 64 * c:off + 64 * c + 64]
                            tensor.matmul(ps, lh, rh, start=(c == 0), stop=False)
                            tensor.matmul(ps, lh, rl, start=False, stop=False)
                            ins = tensor.matmul(ps, ll, rh, start=False,
                                                stop=(c == 7))
                    MM.inc(ins)
                    out_mm.append(MM.n)
                sec("tensor", f_mm)
                return MM.n

            def jit_rope(kind, hh, h, wait_mm):
                bcol = 16 * kind + h
                if kind == 0:
                    dst2, ph, pl = t_qb2[h % 2], 64, 0
                else:
                    dst2, ph, pl = t_kb2, 0, 64
                def f_rope(vector, hh=hh, bcol=bcol, dst2=dst2, ph=ph, pl=pl,
                           wait_mm=wait_mm, kind=kind):
                    c0, c1 = 1024 * hh, 1024 * hh + 1024
                    vector.wait_ge(MM.h, wait_mm)
                    vector.stream_shuffle(t_qs[:], P0[0:64, 0:1024], PAIRSWAP)
                    vector.scalar_tensor_tensor(t_qs[:], t_qs[:],
                                                t_binTsh[:, bcol:bcol + 1],
                                                t_ssin[:, c0:c1], OP.add, OP.mult)
                    vector.scalar_tensor_tensor(t_qr[:], P0[0:64, 0:1024],
                                                t_binT[:, bcol:bcol + 1],
                                                t_cosx[:, c0:c1], OP.add, OP.mult)
                    vector.tensor_add(t_qr[:], t_qr[:], t_qs[:])
                    vector.tensor_copy(t_rhib[:], t_qr[:])
                    vector.tensor_sub(t_rlob[:], t_qr[:], t_rhib[:])
                    vector.tensor_copy(dst2[ph:ph + 64, c0:c1], t_rhib[:])
                    if kind == 1:
                        vector.tensor_copy(dst2[pl:pl + 64, c0:c1], t_rlob[:])
                        vector.tensor_copy(t_kb2r[64:128, c0:c1], t_rhib[:])
                        VV.inc(vector.tensor_copy(t_kb2r[0:64, c0:c1],
                                                  t_rlob[:]))
                    else:
                        VV.inc(vector.tensor_copy(dst2[pl:pl + 64, c0:c1],
                                                  t_rlob[:]))
                sec("vector", f_rope)
                return VV.n

            def jit_v(hh, h, mm_v, ctx_mm):
                def f_vsp(vector, h=h, hh=hh, mmn=mm_v):
                    vector.wait_ge(MM.h, mmn)
                    vector.tensor_scalar(t_qs[:], P0[0:64, 0:1024],
                                         t_binT[:, 32 + h:33 + h], None, OP.add)
                    vector.tensor_copy(t_vnh[:], t_qs[:])
                    VV.inc(vector.tensor_sub(t_vnl[:], t_qs[:], t_vnh[:]))
                sec("vector", f_vsp)
                vv_spl = VV.n
                for vsrc, vdst in ((t_vnh, t_vth), (t_vnl, t_vtl)):
                    def f_vtr(tensor, vsrc=vsrc, vvn=VV.n):
                        tensor.wait_ge(VV.h, vvn)
                        for i8 in range(8):
                            ins = tensor.transpose(
                                P2b[:, 64 * i8:64 * i8 + 64],
                                vsrc[:, 128 * i8:128 * i8 + 128],
                                t_eyeb[0:64, 0:64])
                        MM.inc(ins)
                    sec("tensor", f_vtr)

                    def f_vev(vector, vdst=vdst, hh=hh, mmn=MM.n, ctx_mm=ctx_mm):
                        vector.wait_ge(MM.h, max(mmn, ctx_mm))
                        dst = vdst[:, 65 * 8 * hh:65 * 8 * hh + 520] \
                            .rearrange("p (t f) -> p t f", f=65)[:, :, 0:64]
                        src = P2b[:, 0:512].rearrange("p (t f) -> p t f", f=64)
                        VV.inc(vector.tensor_copy(dst, src))
                    sec("vector", f_vev)
                return VV.n

            def emit_jit_stage(st8, h, state, ctx_mm=0, sc_mm=0):
                if st8 == 0:
                    state["ld_w"] = jit_ldw(h, state.get("last_mm", 0))
                    state["mmqA"] = jit_half_mm(0, 0, state["ld_w"],
                                                state.get("p0_free", 0))
                elif st8 == 1:
                    jit_rope(0, 0, h, state["mmqA"])
                    state["mmqB"] = jit_half_mm(0, 1, state["ld_w"])
                elif st8 == 2:
                    jit_rope(0, 1, h, state["mmqB"])
                    state["mmkA"] = jit_half_mm(1, 0, state["ld_w"])
                elif st8 == 3:
                    def f_kwait(vector, w=sc_mm):
                        pass
                    state["vv_kA"] = jit_rope(1, 0, h, max(state["mmkA"], sc_mm))
                    state["mmkB"] = jit_half_mm(1, 1, state["ld_w"])
                elif st8 == 4:
                    jit_rope(1, 1, h, state["mmkB"])
                    state["mmvA"] = jit_half_mm(2, 0, state["ld_w"])
                elif st8 == 5:
                    jit_v(0, h, max(state["mmvA"], 0), ctx_mm)
                    state["mmvB"] = jit_half_mm(2, 1, state["ld_w"])
                elif st8 == 6:
                    state["p0_free"] = jit_v(1, h, state["mmvB"], ctx_mm)
                    state["last_mm"] = MM.n

            jstate = {}
            for st8 in range(7):
                emit_jit_stage(st8, 0, jstate, ctx_mm=MM.n, sc_mm=MM.n)
            if seq == 0:
                tap_dump("qbh0", t_qb2[0][64:128, :], [(VV, VV.n)])
                tap_dump("kbh0", t_kb2[0:64, :], [(VV, VV.n)])
                tap_dump("vth0", t_vth[:], [(VV, VV.n)])

            for h in range(H):
                nstate = {"last_mm": jstate.get("last_mm", 0),
                          "p0_free": jstate.get("p0_free", 0)}
                aa_h0 = AA.n
                sc_targets = []
                sc_j3g7 = [0]
                ctx_j3g7 = [0]
                for j in range(4):
                    vv0j = VV.n
                    aa0j = aa_h0 + 8 * j
                    mm_sc = {}
                    mm_ctx = {}
                    def f_tens(tensor, j=j, vv0j=vv0j, aa0j=aa0j, h=h,
                               vvh=VV.n, mm_sc=mm_sc, mm_ctx=mm_ctx):
                        for g in range(8):
                            Pg = P1 if g % 2 == 0 else P1b
                            if g < 2:
                                tensor.wait_ge(VV.h, vvh)
                                if aa0j:
                                    tensor.wait_ge(AA.h, aa0j)
                            else:
                                tensor.wait_ge(AA.h, aa0j + g - 1)
                            for k2 in (2 * g, 2 * g + 1):
                                col = 512 * (k2 % 2)
                                ps = Pg[:, col:col + 512]
                                kr = t_kb2r[:, 128 * k2:128 * k2 + 128]
                                k_2 = t_kb2[:, 128 * k2:128 * k2 + 128]
                                q_2 = t_qb2[h % 2][:, 512 * j:512 * j + 512]
                                tensor.matmul(ps, kr, q_2, start=True, stop=False)
                                ins = tensor.matmul(ps, k_2, q_2,
                                                    start=False, stop=True)
                            MM.inc(ins)
                            mm_sc[g] = MM.n
                            if g >= 1:
                                emit_ctx(tensor, j, g - 1, vv0j)
                        emit_ctx(tensor, j, 7, vv0j)
                        tensor.wait_ge(AA.h, aa0j + 8)
                        tensor.wait_ge(VV.h, vv0j + 9)
                        MM.inc(tensor.matmul(P1[0:64, 0:512], t_ones1[0:1, 0:64],
                                             t_rc[:], start=True, stop=True))
                        mm_ctx["rrep"] = MM.n

                    def emit_ctx(tensor, j, g, vv0j):
                        Pg = P1 if g % 2 == 0 else P1b
                        tensor.wait_ge(VV.h, vv0j + g + 1)
                        for k2 in (2 * g, 2 * g + 1):
                            col = 512 * (k2 % 2)
                            vh = t_vth[:, 65 * k2:65 * k2 + 65]
                            vl = t_vtl[:, 65 * k2:65 * k2 + 65]
                            eh = t_Eh[g % 2][:, col:col + 512]
                            el = t_El[g % 2][:, col:col + 512]
                            st = (g == 0 and k2 == 2 * g)
                            sp = (g == 7 and k2 == 2 * g + 1)
                            tensor.matmul(P3[0:65, :], vh, eh, start=st, stop=False)
                            tensor.matmul(P3[0:65, :], vl, eh, start=False, stop=False)
                            ins = tensor.matmul(P3[0:65, :], vh, el,
                                                start=False, stop=sp)
                        MM.inc(ins)
                        mm_ctx[g] = MM.n

                    sec("tensor", f_tens)
                    if j == 3:
                        sc_j3g7[0] = mm_sc[7]
                        ctx_j3g7[0] = mm_ctx[7]

                    for g in range(8):
                        def f_split(vector, g=g, aa0j=aa0j,
                                    mmn=mm_ctx.get(g - 2, 0)):
                            vector.wait_ge(AA.h, aa0j + g + 1)
                            if g >= 2:
                                vector.wait_ge(MM.h, mmn)
                            ep = g % 2
                            vector.tensor_copy(t_Eh[ep][:], t_Ef[ep][:])
                            VV.inc(vector.tensor_sub(t_El[ep][:], t_Ef[ep][:],
                                                     t_Eh[ep][:]))
                        sec("vector", f_split)

                    def f_rc(vector, mmn=mm_ctx[7], sts=ST.snap(), h=h):
                        vector.wait_ge(MM.h, mmn)
                        if h > 0:
                            ST.wait_snap(vector, sts)
                        VV.inc(vector.reciprocal(t_rc[:], P3[64:65, 0:512]))
                    sec("vector", f_rc)

                    def f_norm(vector, j=j, mmn=mm_ctx["rrep"]):
                        vector.wait_ge(MM.h, mmn)
                        vector.tensor_copy(t_rr[:], P1[0:64, 0:512])
                        vector.tensor_mul(t_ctn[:], P3[0:64, 0:512], t_rr[:])
                        vector.tensor_copy(t_cth[:, 512 * j:512 * j + 512],
                                           t_ctn[:])
                        VV.inc(vector.tensor_sub(
                            t_ctl[:, 512 * j:512 * j + 512], t_ctn[:],
                            t_cth[:, 512 * j:512 * j + 512]))
                    sec("vector", f_norm)
                    assert VV.n == vv0j + 10, (VV.n, vv0j)
                    sc_targets.append((j, dict(mm_sc), vv0j))

                    if h + 1 < H:
                        if j == 0:
                            emit_jit_stage(0, h + 1, nstate)
                        elif j == 1:
                            emit_jit_stage(1, h + 1, nstate)
                        elif j == 2:
                            emit_jit_stage(2, h + 1, nstate)

                if h + 1 < H:
                    emit_jit_stage(3, h + 1, nstate, sc_mm=sc_j3g7[0])
                    emit_jit_stage(4, h + 1, nstate)
                    emit_jit_stage(5, h + 1, nstate, ctx_mm=ctx_j3g7[0])
                    emit_jit_stage(6, h + 1, nstate, ctx_mm=ctx_j3g7[0])
                jstate = nstate

                def f_exps(scalar, targets=tuple(sc_targets)):
                    for j, msc, vv0j in targets:
                        for g in range(8):
                            scalar.wait_ge(MM.h, msc[g])
                            if g >= 2:
                                scalar.wait_ge(VV.h, vv0j + g - 1)
                            elif vv0j + g - 3 > 0:
                                scalar.wait_ge(VV.h, vv0j + g - 3)
                            Pg = P1 if g % 2 == 0 else P1b
                            AA.inc(scalar.activation(t_Ef[g % 2][:], Pg[:],
                                                     AF.Exp, scale=0.125))
                sec("scalar", f_exps)
                assert AA.n == aa_h0 + 32, (AA.n, aa_h0)

                def f_stc(gpsimd, h=h, vvn=VV.n):
                    ST.group(gpsimd)
                    gpsimd.wait_ge(VV.h, vvn)
                    ST.inc(gpsimd.dma_start(out=ctxh_s[h], in_=t_cth[:]), 16)
                    ST.inc(gpsimd.dma_start(out=ctxl_s[h], in_=t_ctl[:]), 16)
                sec("gpsimd", f_stc)
                if h == 0 and seq == 0:
                    tap_dump("cth0", t_cth[:], [(VV, VV.n)])

            barrier()

            def f_ldctx(sync, sts=ST.snap()):
                LD.group(sync)
                ST.wait_snap(sync, sts)
                for c in range(8):
                    LD.inc(sync.dma_start(out=t_xn2h[0:64, S * c:S * c + S],
                                          in_=ctxh_s[2 * c]), 16)
                    LD.inc(sync.dma_start(out=t_xn2h[64:128, S * c:S * c + S],
                                          in_=ctxh_s[2 * c + 1]), 16)
                    LD.inc(sync.dma_start(out=t_xn2l[0:64, S * c:S * c + S],
                                          in_=ctxl_s[2 * c]), 16)
                    LD.inc(sync.dma_start(out=t_xn2l[64:128, S * c:S * c + S],
                                          in_=ctxl_s[2 * c + 1]), 16)
            sec("sync", f_ldctx)
            ld_ctx = LD.tok()

            OPS = [P3[0:128, :], P1[:, 0:512]]
            evo_t = {}
            st_o = {}
            for o2 in range(2):
                def f_ldwo(sync, o2=o2, mmn=MM.n):
                    LD.group(sync)
                    if o2 > 0:
                        sync.wait_ge(MM.h, mmn)
                    for dst, srcw in ((t_wo2h, wouth_s), (t_wo2l, woutl_s)):
                        src = srcw[:, 512 * o2:512 * o2 + 512] \
                            .rearrange("(c p) f -> p c f", p=128)
                        LD.inc(sync.dma_start(out=dst[:], in_=src), 16)
                sec("sync", f_ldwo)
                ld_wo = LD.tok()

                for tt in range(16):
                    ot = 16 * o2 + tt
                    par = ot % 2
                    trow = row0 + 128 * tt
                    def f_ldxn(sync, trow=trow, o2=o2, par=par,
                               vvp=evo_t.get(16 * o2 + tt - 2, 0),
                               vv0=VV.n if (16 * o2 + tt) < 2 else 0,
                               sts=ST.snap() if (16 * o2 + tt) < 2 else None):
                        LD.group(sync)
                        prev = vvp or vv0
                        if prev:
                            sync.wait_ge(VV.h, prev)
                        if sts is not None:
                            ST.wait_snap(sync, sts)
                        LD.inc(sync.dma_start(
                            out=t_xnre[par][:],
                            in_=xn_s[trow:trow + 128, 512 * o2:512 * o2 + 512]), 16)
                    sec("sync", f_ldxn)
                    ld_xn = LD.tok()

                    def f_mmo(tensor, tt=tt, o2=o2, par=par, ldn=ld_wo,
                              ldc=ld_ctx, ldx=ld_xn,
                              vvp=evo_t.get(16 * o2 + tt - 2, 0),
                              vv0=VV.n if (16 * o2 + tt) < 2 else 0):
                        LD.wait(tensor, ldn)
                        LD.wait(tensor, ldc)
                        LD.wait(tensor, ldx)
                        prev = vvp or vv0
                        if prev:
                            tensor.wait_ge(VV.h, prev)
                        ps = OPS[par]
                        tensor.matmul(ps, t_ones1[:],
                                      t_boutr[0:1, 512 * o2:512 * o2 + 512],
                                      start=True, stop=False)
                        for c in range(8):
                            lh = t_xn2h[:, S * c + 128 * tt:S * c + 128 * tt + 128]
                            ll = t_xn2l[:, S * c + 128 * tt:S * c + 128 * tt + 128]
                            rh = t_wo2h[:, 512 * c:512 * c + 512]
                            rl = t_wo2l[:, 512 * c:512 * c + 512]
                            tensor.matmul(ps, lh, rh, start=False, stop=False)
                            tensor.matmul(ps, lh, rl, start=False, stop=False)
                            ins = tensor.matmul(ps, ll, rh, start=False,
                                                stop=(c == 7))
                        MM.inc(ins)
                    sec("tensor", f_mmo)

                    def f_evo(vector, par=par, mmn=MM.n, ldx=ld_xn,
                              stp=st_o.get(16 * o2 + tt - 2),
                              sts=ST.snap() if (16 * o2 + tt) < 2 else None):
                        vector.wait_ge(MM.h, mmn)
                        LD.wait(vector, ldx)
                        if stp:
                            ST.wait(vector, stp)
                        elif sts is not None:
                            ST.wait_snap(vector, sts)
                        VV.inc(vector.tensor_add(t_sout[par][:], OPS[par],
                                                 t_xnre[par][:]))
                    sec("vector", f_evo)
                    evo_t[ot] = VV.n

                    def f_sts(gpsimd, trow=trow, o2=o2, par=par, vvn=VV.n):
                        ST.group(gpsimd)
                        gpsimd.wait_ge(VV.h, vvn)
                        ST.inc(gpsimd.dma_start(
                            out=s_s[trow:trow + 128, 512 * o2:512 * o2 + 512],
                            in_=t_sout[par][:]), 16)
                    sec("gpsimd", f_sts)
                    st_o[ot] = ST.tok()

            if seq == 0:
                tap_dump("seq0done", t_sout[1][:], [(VV, VV.n)])
            barrier()

        stA.close()
        barrier()

        stB = ExitStack()
        t_xnc = [sb("t_xnc0", [128, D], F32, stB), sb("t_xnc1", [128, D], F32, stB)]
        t_fq = sb("t_fq", [128, 32], F32, stB)
        t_y = sb("t_y", [128, 2048], F32, stB)
        t_i32 = sb("t_i32", [128, 2048], I32, stB)
        t_c01 = sb("t_c01", [128, 1024], F32, stB)
        t_xq = sb("t_xq", [128, D], BF16, stB)
        t_xqT = sb("t_xqT", [128, 8 * 512], BF16, stB)
        t_h = [sb("t_h%d" % i, [128, INNER], F32, stB) for i in range(4)]
        t_hq = sb("t_hq", [128, INNER], BF16, stB)
        t_hqT = [sb("t_hqT%d" % i, [128, INNER], BF16, stB) for i in range(4)]
        t_wgb = [sb("t_wgb0", [128, 4096], BF16, stB), sb("t_wgb1", [128, 4096], BF16, stB)]
        t_wub = [sb("t_wub0", [128, 4096], BF16, stB), sb("t_wub1", [128, 4096], BF16, stB)]
        t_wdb = [sb("t_wdb%d" % i, [128, 512], BF16, stB) for i in range(5)]
        t_sg = [sb("t_sg0", [128, 512], F32, stB), sb("t_sg1", [128, 512], F32, stB)]
        t_gt = [sb("t_gt0", [128, 512], F32, stB), sb("t_gt1", [128, 512], F32, stB)]
        t_sch = [sb("t_sch0", [128, 512], F32, stB), sb("t_sch1", [128, 512], F32, stB)]
        t_Xch = [sb("t_Xch0", [128, 512], F32, stB), sb("t_Xch1", [128, 512], F32, stB)]
        t_oth = [sb("t_oth0", [128, 512], F32, stB), sb("t_oth1", [128, 512], F32, stB)]


        N_CC = TOK // 512

        def emit_xnquant(cc):
            crow = 512 * cc
            for tt in range(4):
                par = tt % 2
                def f_ldn(sync, tt=tt, par=par, crow=crow, vvn=VV.n, cc=cc):
                    LD.group(sync)
                    if cc > 0 or tt >= 2:
                        sync.wait_ge(VV.h, vvn)
                    LD.inc(sync.dma_start(
                        out=t_xnc[par][:],
                        in_=xn_s[crow + 128 * tt:crow + 128 * tt + 128, :]), 16)
                sec("sync", f_ldn)

                def f_q(vector, tt=tt, par=par, ldt=LD.tok(), mmn=MM.n, cc=cc):
                    LD.wait(vector, ldt)
                    if cc > 0 or tt > 0:
                        vector.wait_ge(MM.h, mmn)
                    xc = t_xnc[par][:]
                    cb = 8 * tt
                    vector.reduce_max(t_fq[:, cb:cb + 1], xc, AX,
                                      apply_absolute_value=True)
                    vector.tensor_single_scalar(t_fq[:, cb:cb + 1],
                                                t_fq[:, cb:cb + 1], 1e-5, OP.max)
                    vector.reciprocal(t_fq[:, cb + 1:cb + 2], t_fq[:, cb:cb + 1])
                    vector.tensor_single_scalar(t_fq[:, cb + 1:cb + 2],
                                                t_fq[:, cb + 1:cb + 2], 127.0,
                                                OP.mult)
                    vector.tensor_single_scalar(t_fq[:, cb + 2:cb + 3],
                                                t_fq[:, cb:cb + 1], 1.0 / 127.0,
                                                OP.mult)
                    vector.tensor_mul(t_fq[:, cb + 3:cb + 4], t_fq[:, cb + 2:cb + 3],
                                      t_scal[:, 1:2])
                    vector.tensor_mul(t_fq[:, cb + 4:cb + 5], t_fq[:, cb + 2:cb + 3],
                                      t_scal[:, 3:4])
                    vector.tensor_scalar(t_y[:, 0:1024], xc, t_fq[:, cb + 1:cb + 2],
                                         512.0, OP.mult, OP.add)
                    vector.tensor_copy(t_i32[:, 0:1024], t_y[:, 0:1024])
                    vector.tensor_sub(t_y[:, 0:1024], t_y[:, 0:1024], t_i32[:, 0:1024])
                    vector.tensor_single_scalar(t_c01[:, 0:1024], t_y[:, 0:1024],
                                                0.5, OP.is_gt)
                    VV.inc(vector.scalar_tensor_tensor(t_xq[:], t_i32[:, 0:1024],
                                                       -512.0,
                                                       t_c01[:, 0:1024], OP.add, OP.add))
                sec("vector", f_q)
                if cc == 0 and tt == 0:
                    tap_dump("xq0", t_xq[:], [(VV, VV.n)])

                def f_trx(tensor, tt=tt, vvn=VV.n, vve=VV.n):
                    tensor.wait_ge(VV.h, vvn)
                    for c in range(8):
                        ins = tensor.transpose(P2b[:, 128 * c:128 * c + 128],
                                               t_xq[:, 128 * c:128 * c + 128],
                                               t_eyeb[:])
                    MM.inc(ins)
                sec("tensor", f_trx)

                def f_evx(vector, tt=tt, mmn=MM.n):
                    vector.wait_ge(MM.h, mmn)
                    dst = t_xqT[:].rearrange("p (c r) -> p c r", r=512)[
                        :, :, 128 * tt:128 * tt + 128]
                    src = P2b[:].rearrange("p (c r) -> p c r", r=128)
                    VV.inc(vector.tensor_copy(dst, src))
                sec("vector", f_evx)

        emit_xnquant(0)
        prev_gmm = {}
        prev_dmm = {}
        for cc in range(N_CC):
            crow = 512 * cc
            aa_g0 = AA.n
            vv_g0 = VV.n
            g_mm = {}
            for jj in range(8):
                jp = jj % 2
                def f_ldw(sync, jj=jj, jp=jp, cc=cc,
                          mmn=(g_mm[4 * (jj - 2) + 3] if jj >= 2 else
                               prev_gmm.get(jj % 2, 0))):
                    LD.group(sync)
                    if mmn:
                        sync.wait_ge(MM.h, mmn)
                    for wsrc, dstt in ((wgT_s, t_wgb[jp]), (wuT_s, t_wub[jp])):
                        LD.inc(sync.dma_start(out=dstt[:], in_=wsrc[jj]), 16)
                sec("sync", f_ldw)
                ld_wj = LD.tok()

                for tt in range(4):
                    idx = 4 * jj + tt
                    def f_gmm(tensor, jj=jj, jp=jp, tt=tt, idx=idx, ldt=ld_wj,
                              vvn=(VV.n if idx < 2 else 0), vv_g0=vv_g0,
                              aa_g0=aa_g0, aap=AA.n):
                        LD.wait(tensor, ldt)
                        if idx < 2:
                            tensor.wait_ge(VV.h, vvn)
                        else:
                            tensor.wait_ge(VV.h, vv_g0 + idx - 1)
                        if idx >= 2:
                            tensor.wait_ge(AA.h, aa_g0 + idx - 1)
                        elif aap:
                            tensor.wait_ge(AA.h, aap)
                        gps = P3[0:128, :] if tt % 2 == 0 else P0[:, 0:512]
                        for c in range(8):
                            lx = t_xqT[:, 512 * c + 128 * tt:512 * c + 128 * tt + 128]
                            tensor.matmul(gps, lx,
                                          t_wgb[jp][:, 512 * c:512 * c + 512],
                                          start=(c == 0), stop=(c == 7))
                        for c in range(8):
                            lx = t_xqT[:, 512 * c + 128 * tt:512 * c + 128 * tt + 128]
                            ins = tensor.matmul(
                                P1[:, 512 * (tt % 2):512 * (tt % 2) + 512], lx,
                                t_wub[jp][:, 512 * c:512 * c + 512],
                                start=(c == 0), stop=(c == 7))
                        MM.inc(ins)
                    sec("tensor", f_gmm)
                    g_mm[idx] = MM.n

                    def f_h(vector, jj=jj, tt=tt, idx=idx, aa_g0=aa_g0):
                        vector.wait_ge(AA.h, aa_g0 + idx + 1)
                        VV.inc(vector.scalar_tensor_tensor(
                            t_h[tt][:, 512 * jj:512 * jj + 512],
                            P1[:, 512 * (tt % 2):512 * (tt % 2) + 512],
                            t_fq[:, 8 * tt + 4:8 * tt + 5], t_sg[tt % 2][:],
                            OP.mult, OP.mult))
                    sec("vector", f_h)

            prev_gmm = {0: g_mm[4 * 6 + 3], 1: g_mm[4 * 7 + 3]}

            def f_silus(scalar, g_mm=dict(g_mm), vv_g0=vv_g0, aa_g0=aa_g0):
                for idx in range(32):
                    jj, tt = idx // 4, idx % 4
                    scalar.wait_ge(MM.h, g_mm[idx])
                    if idx >= 2:
                        scalar.wait_ge(VV.h, vv_g0 + idx - 1)
                    gps = P3[0:128, :] if tt % 2 == 0 else P0[:, 0:512]
                    AA.inc(scalar.activation(t_sg[tt % 2][:], gps, AF.Silu,
                                             scale=t_fq[:, 8 * tt + 3:8 * tt + 4]))
            sec("scalar", f_silus)
            assert AA.n == aa_g0 + 32
            assert VV.n == vv_g0 + 32

            if cc == 0:
                tap_dump("h00", t_h[0][:], [(VV, VV.n)])

            for tt in range(4):
                def f_hq(vector, tt=tt, mmn=MM.n, cc=cc, aan=AA.n):
                    cb = 8 * tt
                    if cc > 0:
                        vector.wait_ge(MM.h, mmn)
                    if tt == 0:
                        vector.wait_ge(AA.h, aan)
                    vector.reduce_max(t_fq[:, cb + 5:cb + 6], t_h[tt][:], AX,
                                      apply_absolute_value=True)
                    vector.tensor_single_scalar(t_fq[:, cb + 5:cb + 6],
                                                t_fq[:, cb + 5:cb + 6], 1e-5, OP.max)
                    vector.reciprocal(t_fq[:, cb + 6:cb + 7], t_fq[:, cb + 5:cb + 6])
                    vector.tensor_single_scalar(t_fq[:, cb + 6:cb + 7],
                                                t_fq[:, cb + 6:cb + 7], 127.0,
                                                OP.mult)
                    vector.tensor_single_scalar(t_fq[:, cb + 5:cb + 6],
                                                t_fq[:, cb + 5:cb + 6], 1.0 / 127.0,
                                                OP.mult)
                    vector.tensor_mul(t_fq[:, cb + 7:cb + 8], t_fq[:, cb + 5:cb + 6],
                                      t_scal[:, 5:6])
                    for qq in range(2):
                        hsl = t_h[tt][:, 2048 * qq:2048 * qq + 2048]
                        vector.tensor_scalar(t_y[:], hsl, t_fq[:, cb + 6:cb + 7],
                                             512.0, OP.mult, OP.add)
                        vector.tensor_copy(t_i32[:], t_y[:])
                        vector.tensor_sub(t_y[:], t_y[:], t_i32[:])
                        for hh in range(2):
                            ys = t_y[:, 1024 * hh:1024 * hh + 1024]
                            vector.tensor_single_scalar(t_c01[:], ys, 0.5, OP.is_gt)
                            ins = vector.scalar_tensor_tensor(
                                t_hq[:, 2048 * qq + 1024 * hh:
                                     2048 * qq + 1024 * hh + 1024],
                                t_i32[:, 1024 * hh:1024 * hh + 1024],
                                -512.0, t_c01[:], OP.add, OP.add)
                    VV.inc(ins)
                sec("vector", f_hq)
                if cc == 0 and tt == 0:
                    tap_dump("hq0", t_hq[:], [(VV, VV.n)])

                for grp in range(4):
                    def f_trh(tensor, tt=tt, grp=grp, vvn=VV.n):
                        tensor.wait_ge(VV.h, vvn)
                        for c in range(8):
                            ci = 8 * grp + c
                            ins = tensor.transpose(P2b[:, 128 * c:128 * c + 128],
                                                   t_hq[:, 128 * ci:128 * ci + 128],
                                                   t_eyeb[:])
                        MM.inc(ins)
                    sec("tensor", f_trh)

                    def f_evh(vector, tt=tt, grp=grp, mmn=MM.n):
                        vector.wait_ge(MM.h, mmn)
                        dst = t_hqT[tt][:, 1024 * grp:1024 * grp + 1024]
                        VV.inc(vector.tensor_copy(dst, P2b[:]))
                    sec("vector", f_evh)

            vv_hqT = VV.n
            if cc + 1 < N_CC:
                emit_xnquant(cc + 1)
            DPS = [P0[:, 0:512], P0[:, 512:1024], P1b[:, 0:512], P1b[:, 512:1024]]
            aa_d0 = AA.n
            vv_d0 = VV.n
            d_mm = {}
            dmm_t = {}
            for o in range(2):
                for i in range(32):
                    ip = i % 5
                    pk = (o, i - 5) if i >= 5 else ((o - 1, 27 + i) if o > 0 else None)
                    def f_ldd(sync, o=o, i=i, ip=ip, cc=cc,
                              mmp=(dmm_t.get(pk, 0) if pk else
                                   prev_dmm.get((1, 27 + i), 0))):
                        LD.group(sync)
                        if mmp:
                            sync.wait_ge(MM.h, mmp)
                        LD.inc(sync.dma_start(
                            out=t_wdb[ip][:], in_=wdT_s[i, o]), 16)
                    sec("sync", f_ldd)

                    def f_dmm(tensor, o=o, i=i, ip=ip, ldt=LD.tok(), vvn=vv_hqT,
                              aa_d0=aa_d0, aan=AA.n):
                        LD.wait(tensor, ldt)
                        if i == 0:
                            tensor.wait_ge(VV.h, vvn)
                            if o == 0:
                                tensor.wait_ge(AA.h, aan)
                            else:
                                tensor.wait_ge(AA.h, aa_d0 + 4)
                        for tt in range(4):
                            ins = tensor.matmul(
                                DPS[tt],
                                t_hqT[tt][:, 128 * i:128 * i + 128],
                                t_wdb[ip][:], start=(i == 0), stop=(i == 31))
                        MM.inc(ins)
                    sec("tensor", f_dmm)
                    dmm_t[(o, i)] = MM.n
                d_mm[o] = MM.n

                for tt in range(4):
                    par = tt % 2
                    trow = crow + 128 * tt
                    def f_lds(sync, par=par, trow=trow, o=o, vvn=VV.n, tt=tt,
                              cc=cc, sts=ST.snap()):
                        LD.group(sync)
                        if cc > 0 or o > 0 or tt >= 2:
                            sync.wait_ge(VV.h, vvn)
                        ST.wait_snap(sync, sts)
                        LD.inc(sync.dma_start(
                            out=t_sch[par][:],
                            in_=s_s[trow:trow + 128, 512 * o:512 * o + 512]), 16)
                        LD.inc(sync.dma_start(
                            out=t_Xch[par][:],
                            in_=x[trow:trow + 128, 512 * o:512 * o + 512]), 16)
                    sec("sync", f_lds)
                    ld_sx = LD.tok()

                    def f_comb(vector, par=par, tt=tt, o=o, aa_d0=aa_d0,
                               ldt=ld_sx, sts=ST.snap()):
                        vector.wait_ge(AA.h, aa_d0 + 4 * o + tt + 1)
                        LD.wait(vector, ldt)
                        ST.wait_snap(vector, sts)
                        vector.tensor_mul(t_oth[par][:], t_sch[par][:], t_gt[tt % 2][:])
                        VV.inc(vector.tensor_add(t_oth[par][:], t_oth[par][:],
                                                 t_Xch[par][:]))
                    sec("vector", f_comb)

                    def f_sto(gpsimd, par=par, trow=trow, o=o, vvn=VV.n):
                        ST.group(gpsimd)
                        gpsimd.wait_ge(VV.h, vvn)
                        ST.inc(gpsimd.dma_start(
                            out=out[trow:trow + 128, 512 * o:512 * o + 512],
                            in_=t_oth[par][:]), 16)
                    sec("gpsimd", f_sto)

            prev_dmm = dict(dmm_t)

            def f_dsilu(scalar, d_mm=dict(d_mm), aa_d0=aa_d0, vv_d0=vv_d0, cc=cc):
                for o in range(2):
                    for tt in range(4):
                        scalar.wait_ge(MM.h, d_mm[o])
                        idx = 4 * o + tt
                        if idx >= 2:
                            scalar.wait_ge(VV.h, vv_d0 + 8 * 0 + idx - 1)
                        AA.inc(scalar.activation(
                            t_gt[tt % 2][:], DPS[tt], AF.Silu,
                            scale=t_fq[:, 8 * tt + 7:8 * tt + 8]))
            sec("scalar", f_dsilu)
            assert AA.n == aa_d0 + 8

        barrier()

        stB.close()

        if tapname in ("xns", "sss"):
            src = {"xns": xn_s, "sss": s_s}[tapname]
            def f_tapend(gpsimd, sts=ST.snap(), src=src):
                ST.group(gpsimd)
                ST.wait_snap(gpsimd, sts)
                ST.inc(gpsimd.dma_start(out=dbg[:], in_=src[:]), 16)
            sec("gpsimd", f_tapend)

        def f_final(gpsimd, sts=ST.snap()):
            ST.wait_snap(gpsimd, sts)
        sec("gpsimd", f_final)

    return nc


_CACHE = {}


def kernel(**inputs):
    consts = _host_consts()
    if "nc" not in _CACHE:
        _CACHE["nc"] = build_program()
    nc = _CACHE["nc"]
    X = np.ascontiguousarray(np.asarray(inputs["L0_attn_X"], np.float32))
    base = {
        "w_in": np.ascontiguousarray(np.asarray(inputs["L0_attn_W_in"], np.float32)),
        "b_in": np.ascontiguousarray(np.asarray(inputs["L0_attn_b_in"], np.float32)),
        "w_out": np.ascontiguousarray(np.asarray(inputs["L0_attn_W_out"], np.float32)),
        "b_out": np.ascontiguousarray(np.asarray(inputs["L0_attn_b_out"], np.float32)),
        "gamma": np.ascontiguousarray(np.asarray(inputs["L0_attn_gamma"], np.float32)),
        "beta": np.ascontiguousarray(np.asarray(inputs["L0_attn_beta"], np.float32)),
        "wg": np.ascontiguousarray(np.asarray(inputs["L0_ffn_w_gate"], np.float32)),
        "wu": np.ascontiguousarray(np.asarray(inputs["L0_ffn_w_up"], np.float32)),
        "wd": np.ascontiguousarray(np.asarray(inputs["L0_ffn_w_down"], np.float32)),
    }
    base.update(consts)
    in_maps = []
    for c in range(NCORES):
        m = dict(base)
        m["x"] = np.ascontiguousarray(X[2 * c:2 * c + 2].reshape(TOK, D))
        in_maps.append(m)
    res = run_bass_kernel_spmd(nc, in_maps, core_ids=list(range(NCORES)))
    outs = [np.asarray(r["out"]).reshape(NSEQ, S, D) for r in res.results]
    return np.ascontiguousarray(np.concatenate(outs, axis=0).astype(np.float32))
